# version 38
# baseline (speedup 1.0000x reference)
import numpy as np
import ml_dtypes
from contextlib import ExitStack

import concourse.bass as bass
import concourse.mybir as mybir
from concourse.bass_utils import run_bass_kernel_spmd

F32 = mybir.dt.float32
BF16 = mybir.dt.bfloat16
AF = mybir.ActivationFunctionType
ALU = mybir.AluOpType
AX = mybir.AxisListType

S = 2048
D = 4096
HQ, HKV, HD = 32, 8, 128
FF = 11008
FQ = 2752
DKV = HKV * HD
NKC = D // 128
NT = S // 128
NG = S // 512
THETA = 10000.0
EPS = 1e-5

ENGINES = ("pe", "act", "dve", "pool", "sp")


class Sem:
    __slots__ = ("name", "n", "h")
    def __init__(self, name):
        self.name = name
        self.n = 0
        self.h = None


class Builder:
    def __init__(self):
        self.ops = {k: [] for k in ENGINES}
        self.sems = []

    def sem(self, name):
        s = Sem(name)
        self.sems.append(s)
        return s

    def op(self, engine, fn=None, waits=(), inc=None, inc_amt=1):
        w = [(s, v) for (s, v) in waits if v > 0]
        if inc is not None:
            inc.n += inc_amt
        self.ops[engine].append((fn, w, inc, inc_amt))
        return inc.n if inc is not None else None

    def barrier(self, bar):
        target = bar.n + 5
        bar.n = target
        for e in ENGINES:
            self.ops[e].append((None, [], bar, 1))
            self.ops[e].append((None, [(bar, target)], None, 1))

    def replay(self, engine, eng):
        for fn, waits, inc, amt in self.ops[engine]:
            for s, v in waits:
                eng.wait_ge(s.h, v)
            ins = fn(eng) if fn is not None else None
            if inc is not None:
                if ins is None:
                    eng.sem_inc(inc.h, amt)
                else:
                    ins.then_inc(inc.h, amt)


def build_program(dbg=False, phases=99):
    nc = bass.Bass("TRN2", target_bir_lowering=False)
    b = Builder()

    def din(name, shape, dt=F32):
        return nc.dram_tensor(name, list(shape), dt, kind="ExternalInput").ap()

    x_in = din("x", (S, D))
    xT_in = din("xT", (D, S))
    Wq = din("Wq", (D, D)); Wk = din("Wk", (D, DKV)); Wv = din("Wv", (D, DKV))
    Wo = din("Wo", (D, D))
    w1qT = din("w1qT", (D // 4, FF)); w3qT = din("w3qT", (D // 4, FF))
    w2qT = din("w2qT", (FQ, D))
    a1 = din("a1", (2, D)); a3 = din("a3", (2, D)); a2 = din("a2", (2, FF))
    b1T = din("b1T", (2, FF)); b3T = din("b3T", (2, FF)); b2T = din("b2T", (2, D))
    identc = din("identc", (128, 128), BF16)
    maskb = din("maskb", (128, 128), BF16)
    rkc = din("rkc", (S, 512), BF16)
    rks = din("rks", (S, 512), BF16)
    rqct = din("rqct", (S, 512), BF16)
    rqst = din("rqst", (S, 512), BF16)

    y_out = nc.dram_tensor("y", [S, D], F32, kind="ExternalOutput").ap()

    dkind = "ExternalOutput" if dbg else "Internal"
    q_d = nc.dram_tensor("q_d", [S, D], BF16, kind=dkind).ap()
    k_d = nc.dram_tensor("k_d", [S, DKV], BF16, kind=dkind).ap()
    v_d = nc.dram_tensor("v_d", [HKV, S, HD], BF16, kind=dkind).ap()
    at_d = nc.dram_tensor("at_d", [S, D], BF16, kind=dkind).ap()
    wo_d = nc.dram_tensor("wo_d", [S, D], F32, kind=dkind).ap()
    x1_d = nc.dram_tensor("x1_d", [S, D], BF16, kind=dkind).ap()
    f_d = nc.dram_tensor("f_d", [S, D], F32, kind=dkind).ap()
    hf_dbg = nc.dram_tensor("hf_dbg", [S, FQ], BF16, kind=dkind).ap()
    x1f_dbg = nc.dram_tensor("x1f_dbg", [8, 128, S], BF16, kind=dkind).ap()
    xab_dbg = nc.dram_tensor("xab_dbg", [4, S], BF16, kind=dkind).ap()
    xaa_dbg2 = nc.dram_tensor("xaa_dbg2", [128, NT * 44], F32, kind=dkind).ap()
    arep_dbg = nc.dram_tensor("arep_dbg", [2, 128, 1024], BF16, kind=dkind).ap()

    sem_s = b.sem("s")
    s_pe = b.sem("pe")
    s_act = b.sem("act")
    s_dve = b.sem("dve")
    s_pool = b.sem("pool")
    bar = b.sem("bar")
    s_const = b.sem("cst")
    s_xfm = b.sem("xfm")
    s_xg = [b.sem(f"xg{i}") for i in range(4)]
    swb = [b.sem(f"swb{i}") for i in range(3)]
    sst = [b.sem(f"sst{i}") for i in range(2)]
    stab = [b.sem(f"stab{i}") for i in range(2)]
    sqr = [b.sem(f"sqr{i}") for i in range(2)]
    skv = [b.sem(f"skv{i}") for i in range(2)]
    sld3 = [b.sem(f"sld3{i}") for i in range(2)]
    sx1s = [b.sem(f"sx1s{i}") for i in range(2)]
    s4w = [b.sem(f"s4w{i}") for i in range(2)]
    s6w = [b.sem(f"s6w{i}") for i in range(2)]
    sb2 = b.sem("sb2")
    sld6 = [b.sem(f"sld6{i}") for i in range(2)]
    so6 = [b.sem(f"so6{i}") for i in range(2)]
    sa13 = b.sem("sa13")
    sx3 = [b.sem(f"sx3_{i}") for i in range(2)]
    sx6 = [b.sem(f"sx6_{i}") for i in range(2)]

    es = ExitStack()

    id_sb = es.enter_context(nc.sbuf_tensor("id_sb", [128, 128], BF16))
    maskb_sb = es.enter_context(nc.sbuf_tensor("maskb_sb", [128, 128], BF16))

    for dst, src in ((id_sb, identc), (maskb_sb, maskb)):
        def f(g, dst=dst, src=src):
            return g.dma_start(out=dst[:], in_=src)
        b.op("pool", f, inc=s_const, inc_amt=16)
    for e_ in ("pe", "dve", "pool"):
        b.op(e_, None, waits=[(s_const, 32)])
    eps_sb = es.enter_context(nc.sbuf_tensor("eps_sb", [128, 1], F32))
    def f_eps(v):
        return v.memset(eps_sb[:], EPS)
    b.op("dve", f_eps, inc=s_dve)
    r1_all = es.enter_context(nc.sbuf_tensor("r1_all", [128, NT], F32))
    m1_all = es.enter_context(nc.sbuf_tensor("m1_all", [128, NT], F32))
    r2_all = es.enter_context(nc.sbuf_tensor("r2_all", [128, NT], F32))
    m2_all = es.enter_context(nc.sbuf_tensor("m2_all", [128, NT], F32))

    with ExitStack() as ph:
        x_fm = ph.enter_context(nc.sbuf_tensor("x_fm", [128, NKC * S], BF16))
        stb = [ph.enter_context(nc.sbuf_tensor(f"stb{i}", [128, 16 * 512], BF16))
               for i in range(3)]
        qst = [ph.enter_context(nc.sbuf_tensor(f"qst{i}", [128, 512], BF16))
               for i in range(2)]
        ktc = [ph.enter_context(nc.sbuf_tensor(f"ktc{i}", [128, 512], BF16))
               for i in range(2)]
        kts = [ph.enter_context(nc.sbuf_tensor(f"kts{i}", [128, 512], BF16))
               for i in range(2)]
        ksw = ph.enter_context(nc.sbuf_tensor("ksw", [128, 512], F32))
        kt1 = ph.enter_context(nc.sbuf_tensor("kt1", [128, 512], BF16))
        ps = [ph.enter_context(nc.psum_tensor(f"p1_{i}", [128, 512], F32))
              for i in range(4)]

        def emit_half(j):
            w_ap, kind, c = stripes[j // 2]
            h = j % 2
            buf = stb[j % 3]
            waits = []
            if j - 3 in buf_free:
                waits.append(buf_free[j - 3])
            def f(g, w_ap=w_ap, c=c, h=h, buf=buf):
                src = w_ap[h * 2048:(h + 1) * 2048, c * 512:(c + 1) * 512] \
                    .rearrange("(kc p) c -> p kc c", p=128)
                dst = buf.ap().rearrange("p (kc c) -> p kc c", c=512)
                return g.dma_start(out=dst, in_=src)
            b.op("pool", f, waits=waits, inc=swb[j % 3], inc_amt=16)
            halves_loaded[j] = swb[j % 3].n

        stripes = [(Wq, 'q', c) for c in range(8)] + \
                  [(Wk, 'k', c) for c in range(2)] + \
                  [(Wv, 'v', c) for c in range(2)]

        halves_loaded = {}
        buf_free = {}
        for kc in range(NKC):
            def f(g, kc=kc):
                return g.dma_start(out=x_fm[:, kc * S:(kc + 1) * S],
                                   in_=xT_in[kc * 128:(kc + 1) * 128, :])
            b.op("pool", f, inc=s_xg[kc // 8], inc_amt=16)
            if kc == 3:
                emit_half(0); emit_half(1); emit_half(2)
        gdone = [s_xg[g].n for g in range(4)]
        xfm_all = [(s_xg[g], gdone[g]) for g in range(4)]

        s_ev = b.sem("ev1")
        last_store = {0: 0, 1: 0}
        ktab_free = {}
        kti = 0


        it = 0
        for si, (w_ap, kind, c) in enumerate(stripes):
            for t in range(NT):
                bank = ps[it % 4]
                fine = si <= 3
                waits = ([(s_xg[0], gdone[0])] if fine else list(xfm_all))
                waits += [(swb[(2 * si) % 3], halves_loaded[2 * si]),
                          (swb[(2 * si + 1) % 3], halves_loaded[2 * si + 1])]
                if it >= 4:
                    waits.append((s_ev, it - 3))
                for kc in range(NKC):
                    buf = stb[(2 * si + kc // 16) % 3]
                    def f(te, buf=buf, kc=kc, t=t, bank=bank):
                        return te.matmul(bank[:],
                                         x_fm[:, kc * S + t * 128: kc * S + t * 128 + 128],
                                         buf[:, (kc % 16) * 512:(kc % 16 + 1) * 512],
                                         start=(kc == 0), stop=(kc == NKC - 1))
                    ww = [(s_xg[kc // 8], gdone[kc // 8])] \
                        if (fine and kc % 8 == 0 and kc > 0) else []
                    if kc == 0:
                        b.op("pe", f, waits=waits)
                    elif kc == 15 and t == NT - 1:
                        b.op("pe", f, waits=ww, inc=s_pe)
                        buf_free[2 * si] = (s_pe, s_pe.n)
                    elif kc == NKC - 1:
                        b.op("pe", f, waits=ww, inc=s_pe)
                        if t == NT - 1:
                            buf_free[2 * si + 1] = (s_pe, s_pe.n)
                    else:
                        b.op("pe", f, waits=ww)
                mm_done = s_pe.n

                if kind == 'v':
                    qb = qst[it % 2]
                    w2_ = [(s_pe, mm_done)]
                    if last_store[it % 2] > 0:
                        w2_.append((sst[it % 2], last_store[it % 2]))
                    def f(v, bank=bank, qb=qb):
                        return v.tensor_copy(qb[:], bank[:])
                    b.op("dve", f, waits=w2_, inc=s_ev)
                    ev_ok = s_ev.n
                    def f2(sp, qb=qb, t=t, c=c):
                        dst = v_d[c * 4:(c + 1) * 4, t * 128:(t + 1) * 128, :] \
                            .rearrange("h t d -> t h d")
                        src = qb.ap().rearrange("p (h d) -> p h d", d=128)
                        return sp.dma_start(out=dst, in_=src)
                    b.op("sp", f2, waits=[(s_ev, ev_ok)], inc=sst[it % 2], inc_amt=16)
                    last_store[it % 2] = sst[it % 2].n
                else:
                    tabc, tabs = (rqct, rqst) if kind == 'q' else (rkc, rks)
                    tbi = kti % 2
                    tc, ts_ = ktc[tbi], kts[tbi]
                    wl = []
                    if tbi in ktab_free:
                        wl = [(s_dve, ktab_free[tbi])]
                    def fl1(sp, tc=tc, t=t, tabc=tabc):
                        return sp.dma_start(out=tc[:], in_=tabc[t * 128:(t + 1) * 128, :])
                    b.op("sp", fl1, waits=wl, inc=stab[tbi], inc_amt=16)
                    def fl2(sp, ts_=ts_, t=t, tabs=tabs):
                        return sp.dma_start(out=ts_[:], in_=tabs[t * 128:(t + 1) * 128, :])
                    b.op("sp", fl2, inc=stab[tbi], inc_amt=16)
                    tab_ok = stab[tbi].n
                    qb = qst[it % 2]
                    w2_ = [(s_pe, mm_done), (stab[tbi], tab_ok)]
                    if last_store[it % 2] > 0:
                        w2_.append((sst[it % 2], last_store[it % 2]))
                    def f1(v, bank=bank):
                        return v.tensor_copy(
                            ksw.ap().rearrange("p (h q i) -> p h q i", h=4, q=2, i=64)[:, :, 0:1, :],
                            bank.ap().rearrange("p (h q i) -> p h q i", h=4, q=2, i=64)[:, :, 1:2, :])
                    b.op("dve", f1, waits=w2_)
                    def f2(v, bank=bank):
                        return v.tensor_copy(
                            ksw.ap().rearrange("p (h q i) -> p h q i", h=4, q=2, i=64)[:, :, 1:2, :],
                            bank.ap().rearrange("p (h q i) -> p h q i", h=4, q=2, i=64)[:, :, 0:1, :])
                    b.op("dve", f2, inc=s_dve)
                    sw_ok = s_dve.n
                    def f3(v, bank=bank, tc=tc):
                        return v.tensor_tensor(kt1[:], bank[:], tc[:], ALU.mult)
                    b.op("dve", f3, inc=s_ev)
                    kt_ok = s_ev.n
                    def f4(v, ts_=ts_, qb=qb):
                        return v.tensor_tensor(qb[:], ksw[:], ts_[:], ALU.mult)
                    b.op("dve", f4, waits=[(s_dve, sw_ok)], inc=s_dve)
                    qb_ok = s_dve.n
                    def f5(v, qb=qb):
                        return v.tensor_tensor(qb[:], qb[:], kt1[:], ALU.add)
                    b.op("dve", f5, waits=[(s_dve, qb_ok), (s_ev, kt_ok)], inc=s_dve)
                    ktab_free[tbi] = s_dve.n
                    rope_done = s_dve.n
                    dst_d = q_d if kind == 'q' else k_d
                    def f6(sp, qb=qb, t=t, c=c, dst_d=dst_d):
                        return sp.dma_start(
                            out=dst_d[t * 128:(t + 1) * 128, c * 512:(c + 1) * 512],
                            in_=qb[:])
                    b.op("sp", f6, waits=[(s_dve, rope_done)], inc=sst[it % 2], inc_amt=16)
                    last_store[it % 2] = sst[it % 2].n
                    kti += 1
                it += 1
            nj = 2 * si + 3
            if nj < 2 * len(stripes):
                emit_half(nj)
            nj = 2 * si + 4
            if nj < 2 * len(stripes):
                emit_half(nj)

        b.op("sp", None, waits=[(sst[0], sst[0].n), (sst[1], sst[1].n)])
        b.barrier(bar)

    if phases <= 1:
        return finish(nc, b, es, [])

    with ExitStack() as ph:
        qfm = [ph.enter_context(nc.sbuf_tensor(f"q_fm{i}", [128, HQ * 512], BF16))
               for i in range(2)]
        ksl = [ph.enter_context(nc.sbuf_tensor(f"ksl{i}", [128, S], BF16)) for i in range(2)]
        vsl = [ph.enter_context(nc.sbuf_tensor(f"vsl{i}", [128, S], BF16)) for i in range(2)]
        p_sb = [ph.enter_context(nc.sbuf_tensor(f"p_sb{i}", [128, S], BF16)) for i in range(3)]
        recs = [ph.enter_context(nc.sbuf_tensor(f"recs{i}", [128, 1], F32)) for i in range(4)]
        attn_g = ph.enter_context(nc.sbuf_tensor("attn_g", [128, 4 * D], BF16))
        ones1 = ph.enter_context(nc.sbuf_tensor("ones1", [128, 1], BF16))
        sps2 = ph.enter_context(nc.psum_tensor("sps2", [128, 2048], F32))
        pv_ps = [ph.enter_context(nc.psum_tensor(f"pv_ps{i}", [128, 512], F32))
                 for i in range(2)]
        ssum_ps = [ph.enter_context(nc.psum_tensor(f"ssum_ps{i}", [128, 4], F32))
                   for i in range(2)]

        def fones(v):
            return v.memset(ones1[:], 1.0)
        b.op("dve", fones, inc=s_dve)
        ones_ok = s_dve.n

        nu = 0
        half_exp = {0: 0, 1: 0}
        hp = 0
        kbuf_free = {0: 0, 1: 0}
        att_store_done = 0
        q_ready_h = {}
        pe_end = {-1: 0, -2: 0}

        def emit_qprep(g):
            qf = qfm[g % 2]
            for h in range(HQ):
                wl = []
                if g >= 2 and h == 0:
                    wl = [(s_pe, pe_end[g - 2])]
                def fl(sp, qf=qf, h=h, g=g):
                    return sp.dma_start_transpose(
                        out=qf[:, h * 512:(h + 1) * 512],
                        in_=q_d[g * 512:(g + 1) * 512, h * 128:(h + 1) * 128])
                b.op("sp", fl, waits=wl, inc=sqr[g % 2], inc_amt=16)
                q_ready_h[(g, h)] = sqr[g % 2].n

        emit_qprep(0)
        fo_done = {}
        pv_done_d = {}
        rec_done_d = {}
        for g in range(NG):
            q_fm = qfm[g % 2]
            pend1 = None
            pend2 = None

            def emit_pv(u):
                n_ = u["nu"]; ui = u["ui"]; nkb = u["nkb"]; vsb = u["vsb"]
                pvb = pv_ps[n_ % 2]
                sl = (n_ // 2) % 4
                ssb = ssum_ps[n_ % 2]
                sc_ = (n_ // 2) % 4
                for kb in range(nkb):
                    w_ = []
                    if kb == 0:
                        wv = u["expd"]
                        if (n_ - 2) in fo_done:
                            wv = max(wv, fo_done[n_ - 2])
                        w_ = [(s_act, wv)]
                        w_.append((s_dve, rec_done_d[n_ - 2] if (n_ - 2) in rec_done_d
                                   else ones_ok))
                    def fp(te, kb=kb, ui=ui, vsb=vsb, nkb=nkb, pvb=pvb, sl=sl):
                        return te.matmul(pvb[:, sl * 128:sl * 128 + 128],
                                         p_sb[ui][:, kb * 128:(kb + 1) * 128],
                                         vsb[:, kb * 128:(kb + 1) * 128],
                                         start=(kb == 0), stop=(kb == nkb - 1))
                    b.op("pe", fp, waits=w_)
                    def fs(te, kb=kb, ui=ui, nkb=nkb, ssb=ssb, sc_=sc_):
                        return te.matmul(ssb[:, sc_:sc_ + 1],
                                         p_sb[ui][:, kb * 128:(kb + 1) * 128],
                                         ones1[:],
                                         start=(kb == 0), stop=(kb == nkb - 1))
                    b.op("pe", fs, inc=(s_pe if kb == nkb - 1 else None))
                pv_done_d[n_] = s_pe.n
                def fc(v, n_=n_, ssb=ssb, sc_=sc_):
                    return v.reciprocal(recs[n_ % 4][:], ssb[:, sc_:sc_ + 1])
                b.op("dve", fc, waits=[(s_pe, pv_done_d[n_])], inc=s_dve)
                rec_done_d[n_] = s_dve.n
                w_ = [(s_pe, pv_done_d[n_]), (s_dve, rec_done_d[n_])]
                if u["store_wait"] > 0:
                    w_.append((sem_s, u["store_wait"]))
                def fo(se, qs=u["qs"], h=u["h"], n_=n_, pvb=pvb, sl=sl):
                    return se.activation(
                        attn_g[:, qs * D + h * 128: qs * D + h * 128 + 128],
                        pvb[:, sl * 128:sl * 128 + 128], AF.Copy, bias=0.0,
                        scale=recs[n_ % 4][:, 0:1])
                b.op("act", fo, waits=w_, inc=s_act)
                fo_done[n_] = s_act.n

            for kv in range(HKV):
                ksb, vsb = ksl[kv % 2], vsl[kv % 2]
                wl = []
                if kbuf_free[kv % 2] > 0:
                    wl = [(s_pe, kbuf_free[kv % 2])]
                def fk(sp, ksb=ksb, g=g, kv=kv):
                    return sp.dma_start_transpose(
                        out=ksb[:, 0:(g + 1) * 512],
                        in_=k_d[0:(g + 1) * 512, kv * 128:(kv + 1) * 128])
                b.op("sp", fk, waits=wl, inc=skv[kv % 2], inc_amt=16)
                def fv(sp, vsb=vsb, g=g, kv=kv):
                    src = v_d[kv:kv + 1, 0:(g + 1) * 512, :] \
                        .rearrange("o (kb p) d -> p (o kb) d", p=128)
                    dst = vsb.ap().rearrange("p (kb d) -> p kb d", d=128)[:, 0:(g + 1) * 4, :]
                    return sp.dma_start(out=dst, in_=src)
                b.op("sp", fv, inc=skv[kv % 2], inc_amt=16)
                kv_ok = skv[kv % 2].n
                if kv == 1 and g + 1 < NG:
                    emit_qprep(g + 1)

                for hh in range(4):
                    h = kv * 4 + hh
                    for qs in range(4):
                        qt = g * 4 + qs
                        klen = (qt + 1) * 128
                        nkb = qt + 1
                        ui = nu % 3
                        if klen <= 1024:
                            off0 = hp * 1024
                            occ = [hp]
                            hp ^= 1
                        else:
                            off0 = 0
                            occ = [0, 1]
                        for kb in range(nkb):
                            w_ = []
                            if kb == 0:
                                w_ = [(skv[kv % 2], kv_ok),
                                      (sqr[g % 2], q_ready_h[(g, h)])]
                                for hh_ in occ:
                                    if half_exp[hh_] > 0:
                                        w_.append((s_act, half_exp[hh_]))
                            last_chunk = (kb == nkb - 1)
                            def f(te, kb=kb, h=h, qs=qs, ksb=ksb,
                                  q_fm=q_fm, off0=off0, lc=last_chunk):
                                return te.matmul(
                                    sps2[:, off0 + kb * 128:off0 + kb * 128 + 128],
                                    ksb[:, kb * 128:(kb + 1) * 128],
                                    q_fm[:, h * 512 + qs * 128: h * 512 + qs * 128 + 128],
                                    start=True, stop=(not lc))
                            b.op("pe", f, waits=w_)
                        def fmm(te, off0=off0, qt=qt):
                            return te.matmul(
                                sps2[:, off0 + qt * 128:off0 + qt * 128 + 128],
                                id_sb[:], maskb_sb[:], start=False, stop=True)
                        b.op("pe", fmm, inc=s_pe)
                        s_done = s_pe.n
                        def fe(se, off0=off0, klen=klen, ui=ui):
                            return se.activation(p_sb[ui][:, 0:klen],
                                                 sps2[:, off0:off0 + klen],
                                                 AF.Exp, bias=0.0, scale=1.0)
                        b.op("act", fe, waits=[(s_pe, s_done)], inc=s_act)
                        for hh_ in occ:
                            half_exp[hh_] = s_act.n
                        u_cur = dict(nu=nu, ui=ui, nkb=nkb, vsb=vsb, qs=qs, h=h,
                                     expd=s_act.n,
                                     store_wait=(att_store_done if
                                                 (g > 0 and nu % (HKV * 16) == 0) else 0),
                                     kv=kv, g=g)
                        if pend2 is not None:
                            emit_pv(pend2)
                            if pend2["kv"] != pend1["kv"]:
                                kbuf_free[pend2["kv"] % 2] = s_pe.n
                        pend2 = pend1
                        pend1 = u_cur
                        nu += 1
            if pend2 is not None:
                emit_pv(pend2)
                if pend2["kv"] != pend1["kv"]:
                    kbuf_free[pend2["kv"] % 2] = s_pe.n
                pend2 = None
            if pend1 is not None:
                emit_pv(pend1)
                kbuf_free[pend1["kv"] % 2] = s_pe.n
                pend1 = None
            pe_end[g] = s_pe.n
            att_ev = s_act.n
            for qs in range(4):
                def fs_(sp, g=g, qs=qs):
                    return sp.dma_start(
                        out=at_d[(g * 4 + qs) * 128:(g * 4 + qs + 1) * 128, :],
                        in_=attn_g[:, qs * D:(qs + 1) * D])
                b.op("sp", fs_, waits=[(s_act, att_ev)], inc=sem_s, inc_amt=16)
            att_store_done = sem_s.n
        b.op("sp", None, waits=[(sem_s, sem_s.n)])
        b.barrier(bar)

    if phases <= 2:
        return finish(nc, b, es, [])

    with ExitStack() as ph:
        a_fm = ph.enter_context(nc.sbuf_tensor("a_fm", [128, NKC * S], BF16))
        stb = [ph.enter_context(nc.sbuf_tensor(f"stb3{i}", [128, 16 * 512], BF16))
               for i in range(3)]
        ost = [ph.enter_context(nc.sbuf_tensor(f"ost{i}", [128, 512], F32)) for i in range(2)]
        xr3 = [ph.enter_context(nc.sbuf_tensor(f"xr3_{i}", [128, 512], F32)) for i in range(2)]
        bnstA = ph.enter_context(nc.sbuf_tensor("bnstA", [128, NT * 48], F32))
        bnagA = ph.enter_context(nc.sbuf_tensor("bnagA", [128, 2], F32))
        stdvA = ph.enter_context(nc.sbuf_tensor("stdvA", [128, 1], F32))
        ps = [ph.enter_context(nc.psum_tensor(f"p3_{i}", [128, 512], F32)) for i in range(4)]
        s_ev = b.sem("ev3")

        def emit_xload(it2):
            c2, t2 = it2 // NT, it2 % NT
            wl = []
            if it2 >= 2:
                wl = [(s_ev, xld_free[it2 - 2])]
            def fx(g, c2=c2, t2=t2, bi=it2 % 2):
                return g.dma_start(out=xr3[bi][:],
                                   in_=x_in[t2 * 128:(t2 + 1) * 128,
                                            c2 * 512:(c2 + 1) * 512])
            b.op("pool", fx, waits=wl, inc=sx3[it2 % 2], inc_amt=16)
            xld_ok[it2] = sx3[it2 % 2].n

        xld_ok = {}
        xld_free = {}
        sc_done = {}
        emit_xload(0)
        emit_xload(1)

        def emit_ln1_scalars(tt):
            wl = []
            if tt >= 1:
                wl = [(s_act, sc_done[tt - 1])]
            def fa(v, tt=tt):
                return v.bn_aggr(bnagA[:], bnstA[:, tt * 48:(tt + 1) * 48])
            b.op("dve", fa, waits=wl, inc=s_dve)
            ag_ok = s_dve.n
            def f4(se):
                return se.activation(stdvA[:], bnagA[:, 1:2], AF.Ln,
                                     bias=eps_sb[:, 0:1], scale=1.0)
            b.op("act", f4, waits=[(s_dve, ag_ok)])
            def f4b(se, tt=tt):
                return se.activation(r1_all[:, tt:tt + 1], stdvA[:], AF.Exp,
                                     bias=0.0, scale=-0.5)
            b.op("act", f4b)
            def f4c(se, tt=tt):
                return se.activation(m1_all[:, tt:tt + 1], r1_all[:, tt:tt + 1],
                                     AF.Copy, bias=0.0, scale=bnagA[:, 0:1])
            b.op("act", f4c, inc=s_act)
            sc_done[tt] = s_act.n

        afm_vals = []
        for kc in range(NKC):
            def f(sp, kc=kc):
                return sp.dma_start_transpose(out=a_fm[:, kc * S:(kc + 1) * S],
                                              in_=at_d[:, kc * 128:(kc + 1) * 128])
            b.op("sp", f, inc=s_xfm, inc_amt=16)
            afm_vals.append(s_xfm.n)

        halves_loaded = {}
        buf_free = {}
        def emit_half3(j):
            c, hf = j // 2, j % 2
            buf = stb[j % 3]
            waits = []
            if j - 3 in buf_free:
                waits.append(buf_free[j - 3])
            def f(g, c=c, hf=hf, buf=buf):
                src = Wo[hf * 2048:(hf + 1) * 2048, c * 512:(c + 1) * 512] \
                    .rearrange("(kc p) c -> p kc c", p=128)
                dst = buf.ap().rearrange("p (kc c) -> p kc c", c=512)
                return g.dma_start(out=dst, in_=src)
            b.op("pool", f, waits=waits, inc=swb[j % 3], inc_amt=16)
            halves_loaded[j] = swb[j % 3].n
        emit_half3(0); emit_half3(1); emit_half3(2)

        last_store = {0: sst[0].n, 1: sst[1].n}
        it = 0
        for c in range(8):
            for t in range(NT):
                bank = ps[it % 4]
                waits = [(s_xfm, afm_vals[-1]),
                         (swb[(2 * c) % 3], halves_loaded[2 * c]),
                         (swb[(2 * c + 1) % 3], halves_loaded[2 * c + 1])]
                if it >= 4:
                    waits.append((s_ev, it - 3))
                for kc in range(NKC):
                    buf = stb[(2 * c + kc // 16) % 3]
                    def f(te, buf=buf, kc=kc, t=t, bank=bank):
                        return te.matmul(bank[:],
                                         a_fm[:, kc * S + t * 128: kc * S + t * 128 + 128],
                                         buf[:, (kc % 16) * 512:(kc % 16 + 1) * 512],
                                         start=(kc == 0), stop=(kc == NKC - 1))
                    if kc == 0:
                        b.op("pe", f, waits=waits)
                    elif kc == 15 and t == NT - 1:
                        b.op("pe", f, inc=s_pe)
                        buf_free[2 * c] = (s_pe, s_pe.n)
                    elif kc == NKC - 1:
                        b.op("pe", f, inc=s_pe)
                        if t == NT - 1:
                            buf_free[2 * c + 1] = (s_pe, s_pe.n)
                    else:
                        b.op("pe", f)
                mm_done = s_pe.n
                qb = ost[it % 2]
                w2_ = [(s_pe, mm_done), (sst[it % 2], last_store[it % 2]),
                       (sx3[it % 2], xld_ok[it])]
                def fy(v, bank=bank, qb=qb, bi=it % 2):
                    return v.tensor_tensor(qb[:], xr3[bi][:], bank[:], ALU.add)
                b.op("dve", fy, waits=w2_, inc=s_ev)
                ev_ok = s_ev.n
                xld_free[it] = ev_ok
                def fst_(v, qb=qb, t=t, c=c):
                    return v.bn_stats(bnstA[:, t * 48 + c * 6: t * 48 + c * 6 + 6], qb[:])
                b.op("dve", fst_)
                if it + 2 < 128:
                    emit_xload(it + 2)
                def f2(sp, qb=qb, t=t, c=c):
                    return sp.dma_start(out=wo_d[t * 128:(t + 1) * 128, c * 512:(c + 1) * 512],
                                        in_=qb[:])
                b.op("sp", f2, waits=[(s_ev, ev_ok)], inc=sst[it % 2], inc_amt=16)
                last_store[it % 2] = sst[it % 2].n
                if c == 7 and t >= 1:
                    emit_ln1_scalars(t - 1)
                it += 1
            nj = 2 * c + 3
            if nj < 16:
                emit_half3(nj)
            nj = 2 * c + 4
            if nj < 16:
                emit_half3(nj)
        def fsp3(v):
            return v.tensor_copy(xr3[0][:], xr3[1][:])
        b.op("dve", fsp3)
        emit_ln1_scalars(NT - 1)
        b.op("sp", None, waits=[(sst[0], sst[0].n), (sst[1], sst[1].n)])
        b.barrier(bar)

    if phases <= 3:
        return finish(nc, b, es, [])

    es2 = ExitStack()
    x1f_fm = es2.enter_context(nc.sbuf_tensor("x1f_fm", [128, 8 * S], BF16))
    xa1_bf = es2.enter_context(nc.sbuf_tensor("xa1_bf", [2, S], BF16))
    xa3_bf = es2.enter_context(nc.sbuf_tensor("xa3_bf", [2, S], BF16))

    with ExitStack() as ph:
        wo_sb = [ph.enter_context(nc.sbuf_tensor(f"wo_sb{i}", [128, D], F32)) for i in range(2)]
        x1_sb = [ph.enter_context(nc.sbuf_tensor(f"x1_sb{i}", [128, D], BF16)) for i in range(2)]
        x1f_t2 = [ph.enter_context(nc.sbuf_tensor(f"x1f_t{i}", [128, 1024], BF16)) for i in range(2)]
        x1fm_t2 = [ph.enter_context(nc.sbuf_tensor(f"x1fm_t{i}", [128, D], BF16)) for i in range(2)]
        a1_sb = ph.enter_context(nc.sbuf_tensor("a1_sb", [2, D], BF16))
        a3_sb = ph.enter_context(nc.sbuf_tensor("a3_sb", [2, D], BF16))
        a1T_sb = ph.enter_context(nc.sbuf_tensor("a1T_sb", [128, 64], BF16))
        a3T_sb = ph.enter_context(nc.sbuf_tensor("a3T_sb", [128, 64], BF16))
        xa1_sb = ph.enter_context(nc.sbuf_tensor("xa1_sb", [2, S], F32))
        xa3_sb = ph.enter_context(nc.sbuf_tensor("xa3_sb", [2, S], F32))
        tp_ps = [ph.enter_context(nc.psum_tensor(f"tp_ps{i}", [128, 1024], BF16)) for i in range(2)]
        aT_ps = ph.enter_context(nc.psum_tensor("aT_ps", [128, 128], BF16))
        xa_ps = ph.enter_context(nc.psum_tensor("xa_ps", [2, 512], F32))

        for asb, src in ((a1_sb, a1), (a3_sb, a3)):
            def f(g, asb=asb, src=src):
                return g.dma_start(out=asb[:], in_=src)
            b.op("pool", f, inc=sa13, inc_amt=16)
        a_ok = sa13.n
        for i, (asb, aTsb) in enumerate(((a1_sb, a1T_sb), (a3_sb, a3T_sb))):
            for kc in range(NKC):
                wl = []
                if kc == 0:
                    wl = [(sa13, a_ok)] if i == 0 else [(s_act, s_act.n)]
                def ft(te, asb=asb, kc=kc):
                    return te.transpose(aT_ps[:, kc * 2: kc * 2 + 2],
                                        asb[:, kc * 128:(kc + 1) * 128],
                                        id_sb[0:2, 0:2])
                b.op("pe", ft, waits=wl, inc=(s_pe if kc == NKC - 1 else None))
            trd = s_pe.n
            def fe(se, aTsb=aTsb):
                return se.activation(aTsb[:], aT_ps[:, 0:64], AF.Copy, bias=0.0, scale=1.0)
            b.op("act", fe, waits=[(s_pe, trd)], inc=s_act)

        y_done = {}
        x1_spill = {}
        tr_done = {}
        prev_fe2 = 0
        prev_xa_ev = 0
        for t in range(NT):
            wsb, x1b = wo_sb[t % 2], x1_sb[t % 2]
            wl = []
            if t >= 2:
                wl = [(s_dve, y_done[t - 2]), (sx1s[t % 2], x1_spill[t - 2])]
            def fl1(sp, wsb=wsb, t=t):
                return sp.dma_start(out=wsb[:], in_=wo_d[t * 128:(t + 1) * 128, :])
            b.op("sp", fl1, waits=wl, inc=sld3[t % 2], inc_amt=16)
            ld_ok = sld3[t % 2].n
            y_sb = wsb
            wl6 = [(sld3[t % 2], ld_ok)]
            if t >= 2:
                wl6.append((s_pe, tr_done[t - 2]))
            def f6(v, x1b=x1b, y_sb=y_sb, t=t):
                return v.tensor_scalar(x1b[:], y_sb[:], r1_all[:, t:t + 1],
                                       m1_all[:, t:t + 1],
                                       ALU.mult, ALU.subtract)
            b.op("dve", f6, waits=wl6, inc=s_dve)
            y_done[t] = s_dve.n
            x1_ok = s_dve.n
            def f7(sp, x1b=x1b, t=t):
                return sp.dma_start(out=x1_d[t * 128:(t + 1) * 128, :], in_=x1b[:])
            b.op("sp", f7, waits=[(s_dve, x1_ok)], inc=sx1s[t % 2], inc_amt=16)
            x1_spill[t] = sx1s[t % 2].n
            x1f_t = x1f_t2[t % 2]
            x1fm_t = x1fm_t2[t % 2]
            evs = []
            for kc in range(NKC):
                wv = kc // 8
                bank = tp_ps[wv % 2]
                wl = []
                if kc % 8 == 0:
                    wl = [(s_dve, x1_ok)]
                    if wv >= 2:
                        wl.append((s_act, evs[wv - 2]))
                    elif t > 0 and wv < 2:
                        wl.append((s_act, prev_fe2))
                def ft(te, kc=kc, x1b=x1b, bank=bank):
                    return te.transpose(bank[:, (kc % 8) * 128:(kc % 8 + 1) * 128],
                                        x1b[:, kc * 128:(kc + 1) * 128], id_sb[:])
                b.op("pe", ft, waits=wl, inc=(s_pe if kc % 8 == 7 else None))
                if kc % 8 == 7:
                    trd = s_pe.n
                    def fe(se, wv=wv, bank=bank, x1fm_t=x1fm_t):
                        return se.activation(x1fm_t[:, wv * 1024:(wv + 1) * 1024],
                                             bank[:], AF.Copy, bias=0.0, scale=1.0)
                    b.op("act", fe, waits=[(s_pe, trd)], inc=s_act)
                    evs.append(s_act.n)
            fm_ok = s_act.n
            tr_done[t] = s_pe.n
            for idx, (aTsb, xasb) in enumerate(((a1T_sb, xa1_sb), (a3T_sb, xa3_sb))):
                for kc in range(NKC):
                    wl = []
                    if kc == 0:
                        wl = [(s_act, fm_ok)]
                        if idx == 1 or t > 0:
                            wl.append((s_dve, prev_xa_ev))
                    def fx(te, aTsb=aTsb, kc=kc, x1fm_t=x1fm_t):
                        return te.matmul(xa_ps[:, 0:128],
                                         aTsb[:, kc * 2: kc * 2 + 2],
                                         x1fm_t[:, kc * 128:(kc + 1) * 128],
                                         start=(kc == 0), stop=(kc == NKC - 1))
                    b.op("pe", fx, waits=wl, inc=(s_pe if kc == NKC - 1 else None))
                xd = s_pe.n
                def fxe(v, xasb=xasb, t=t):
                    return v.tensor_copy(xasb[:, t * 128:(t + 1) * 128], xa_ps[:, 0:128])
                b.op("dve", fxe, waits=[(s_pe, xd)], inc=s_dve)
                prev_xa_ev = s_dve.n
            def ff(v, x1b=x1b, x1f_t=x1f_t):
                v.tensor_tensor(x1f_t[:], x1b[:, 0:1024], x1b[:, 1024:2048], ALU.add)
                v.tensor_tensor(x1f_t[:], x1f_t[:], x1b[:, 2048:3072], ALU.add)
                return v.tensor_tensor(x1f_t[:], x1f_t[:], x1b[:, 3072:4096], ALU.add)
            b.op("dve", ff, inc=s_dve)
            fold_ok = s_dve.n
            for jc in range(8):
                wl = []
                if jc == 0:
                    wl = [(s_dve, fold_ok), (s_act, s_act.n)]
                def ft(te, jc=jc, x1f_t=x1f_t):
                    return te.transpose(tp_ps[0][:, jc * 128:(jc + 1) * 128],
                                        x1f_t[:, jc * 128:(jc + 1) * 128], id_sb[:])
                b.op("pe", ft, waits=wl, inc=(s_pe if jc == 7 else None))
            trd = s_pe.n
            def fe2(se, t=t):
                dst = x1f_fm.ap().rearrange("p (jc s) -> p jc s", s=S)[:, :, t * 128:(t + 1) * 128]
                src = tp_ps[0].ap().rearrange("p (jc c) -> p jc c", c=128)
                return se.activation(dst, src, AF.Copy, bias=0.0, scale=1.0)
            b.op("act", fe2, waits=[(s_pe, trd)], inc=s_act)
            prev_fe2 = s_act.n
        xa_all = s_dve.n
        def fc1(se):
            return se.activation(xa1_bf[:], xa1_sb[:], AF.Copy, bias=0.0, scale=0.5)
        b.op("act", fc1, waits=[(s_dve, xa_all)], inc=s_act)
        def fc3(se):
            return se.activation(xa3_bf[:], xa3_sb[:], AF.Copy, bias=0.0, scale=0.5)
        b.op("act", fc3, inc=s_act)
        b.op("sp", None, waits=[(sx1s[0], sx1s[0].n), (sx1s[1], sx1s[1].n)])
        b.barrier(bar)

    if phases <= 4:
        return finish(nc, b, es, [es2])

    es3 = ExitStack()
    h_fold = es3.enter_context(nc.sbuf_tensor("h_fold", [128, NT * FQ], BF16, side="right"))
    pstg = es3.enter_context(nc.sbuf_tensor("pstg", [128, NT * 44], F32, side="right"))

    widths = [512] * 21 + [256]
    with ExitStack() as ph:
        st1 = [ph.enter_context(nc.sbuf_tensor(f"st1_{i}", [128, 8 * 512], BF16)) for i in range(2)]
        st3 = [ph.enter_context(nc.sbuf_tensor(f"st3_{i}", [128, 8 * 512], BF16)) for i in range(2)]
        b1s = [ph.enter_context(nc.sbuf_tensor(f"b1s{i}", [2, 512], BF16)) for i in range(2)]
        b3s = [ph.enter_context(nc.sbuf_tensor(f"b3s{i}", [2, 512], BF16)) for i in range(2)]
        a2r0 = [ph.enter_context(nc.sbuf_tensor(f"a2r0_{i}", [1, 512], BF16)) for i in range(2)]
        a2r1 = [ph.enter_context(nc.sbuf_tensor(f"a2r1_{i}", [1, 512], BF16)) for i in range(2)]
        a2rep = [ph.enter_context(nc.sbuf_tensor(f"a2rep{i}", [128, 1024], BF16)) for i in range(2)]
        ones_sb = ph.enter_context(nc.sbuf_tensor("ones_sb", [1, 128], BF16))
        sg_sb = [ph.enter_context(nc.sbuf_tensor(f"sg_sb{i}", [128, 512], BF16)) for i in range(2)]
        th_sb = [ph.enter_context(nc.sbuf_tensor(f"th_sb{i}", [128, 512], BF16)) for i in range(2)]
        pacc = ph.enter_context(nc.sbuf_tensor("pacc", [128, 1], F32))
        sdump = ph.enter_context(nc.sbuf_tensor("sdump", [128, 1024], BF16))
        g_ps = [ph.enter_context(nc.psum_tensor(f"g_ps{i}", [128, 512], F32)) for i in range(2)]
        u_ps = [ph.enter_context(nc.psum_tensor(f"u_ps{i}", [128, 512], F32)) for i in range(2)]
        ar_ps = ph.enter_context(nc.psum_tensor("ar_ps", [128, 512], F32))

        def fo_(v):
            return v.memset(ones_sb[:], 1.0)
        b.op("dve", fo_, inc=s_dve)
        ones_ok = s_dve.n

        ld_vals = {}
        stripe_used = {}
        def emit_stripe_load(fs):
            w = widths[fs]
            g0 = fs * 512
            bufi = fs % 2
            waits = []
            if fs >= 2:
                waits = [(s_pe, stripe_used[fs - 2])]
            def fA(g, fs=fs, w=w, g0=g0, bufi=bufi):
                src1 = w1qT[:, g0:g0 + w].rearrange("(jc p) c -> p jc c", p=128)
                d1 = st1[bufi].ap().rearrange("p (jc c) -> p jc c", c=512)[:, :, 0:w]
                return g.dma_start(out=d1, in_=src1)
            b.op("pool", fA, waits=waits, inc=s6w[c % 2], inc_amt=16)
            def fB(g, fs=fs, w=w, g0=g0, bufi=bufi):
                src3 = w3qT[:, g0:g0 + w].rearrange("(jc p) c -> p jc c", p=128)
                d3 = st3[bufi].ap().rearrange("p (jc c) -> p jc c", c=512)[:, :, 0:w]
                return g.dma_start(out=d3, in_=src3)
            b.op("pool", fB, inc=s4w[fs % 2], inc_amt=16)
            for dst, src in ((b1s[bufi], b1T), (b3s[bufi], b3T)):
                def fC(g, dst=dst, src=src, g0=g0, w=w):
                    return g.dma_start(out=dst[:, 0:w], in_=src[:, g0:g0 + w])
                b.op("pool", fC, inc=s4w[fs % 2], inc_amt=16)
            def fD(g, bufi=bufi, g0=g0, w=w):
                return g.dma_start(out=a2r0[bufi][:, 0:w], in_=a2[0:1, g0:g0 + w])
            b.op("pool", fD, inc=s4w[fs % 2], inc_amt=16)
            def fE(g, bufi=bufi, g0=g0, w=w):
                return g.dma_start(out=a2r1[bufi][:, 0:w], in_=a2[1:2, g0:g0 + w])
            b.op("pool", fE, inc=s4w[fs % 2], inc_amt=16)
            ld_vals[fs] = s4w[fs % 2].n

        emit_stripe_load(0)
        emit_stripe_load(1)

        silu_done = {}
        h_done = {}
        stt_done = {}
        a2rep_free = {}
        prev_arep_ev = 0
        for fs in range(22):
            w = widths[fs]
            g0 = fs * 512
            bufi = fs % 2
            wl = [(s4w[fs % 2], ld_vals[fs]), (s_dve, ones_ok)]
            if fs >= 2:
                wl.append((s_pool, a2rep_free[fs - 2]))
            if fs >= 1:
                wl.append((s_act, prev_arep_ev))
            def fb(te, bufi=bufi, w=w):
                return te.matmul(ar_ps[:, 0:w], ones_sb[:], a2r0[bufi][:, 0:w],
                                 start=True, stop=True)
            b.op("pe", fb, waits=wl, inc=s_pe)
            bd = s_pe.n
            def fbe(se, bufi=bufi, w=w):
                return se.activation(a2rep[bufi][:, 0:w], ar_ps[:, 0:w], AF.Copy,
                                     bias=0.0, scale=1.0)
            b.op("act", fbe, waits=[(s_pe, bd)], inc=s_act)
            def fb1(te, bufi=bufi, w=w):
                return te.matmul(ar_ps[:, 0:w], ones_sb[:], a2r1[bufi][:, 0:w],
                                 start=True, stop=True)
            b.op("pe", fb1, waits=[(s_act, s_act.n)], inc=s_pe)
            bd = s_pe.n
            def fbe1(se, bufi=bufi, w=w):
                return se.activation(a2rep[bufi][:, 512:512 + w], ar_ps[:, 0:w], AF.Copy,
                                     bias=0.0, scale=1.0)
            b.op("act", fbe1, waits=[(s_pe, bd)], inc=s_act)
            prev_arep_ev = s_act.n
            arep_ok = s_act.n

            for t in range(NT):
                itg = fs * NT + t
                gb, ub = g_ps[t % 2], u_ps[t % 2]
                sg, th = sg_sb[t % 2], th_sb[t % 2]
                wl = [(s4w[fs % 2], ld_vals[fs])]
                if itg >= 2:
                    wl.append((s_act, silu_done[itg - 2]))
                for jc in range(8):
                    def f(te, jc=jc, t=t, w=w, gb=gb, bufi=bufi):
                        return te.matmul(gb[:, 0:w],
                                         x1f_fm[:, jc * S + t * 128: jc * S + t * 128 + 128],
                                         st1[bufi][:, jc * 512: jc * 512 + w],
                                         start=(jc == 0), stop=False)
                    b.op("pe", f, waits=(wl if jc == 0 else []))
                def flo(te, t=t, w=w, gb=gb, bufi=bufi):
                    return te.matmul(gb[:, 0:w], xa1_bf[:, t * 128:(t + 1) * 128],
                                     b1s[bufi][:, 0:w], start=False, stop=True)
                b.op("pe", flo, inc=s_pe)
                g_done = s_pe.n
                wl = []
                if itg >= 2:
                    wl = [(s_dve, h_done[itg - 2])]
                for jc in range(8):
                    def f(te, jc=jc, t=t, w=w, ub=ub, bufi=bufi):
                        return te.matmul(ub[:, 0:w],
                                         x1f_fm[:, jc * S + t * 128: jc * S + t * 128 + 128],
                                         st3[bufi][:, jc * 512: jc * 512 + w],
                                         start=(jc == 0), stop=False)
                    b.op("pe", f, waits=(wl if jc == 0 else []))
                def flo3(te, t=t, w=w, ub=ub, bufi=bufi):
                    return te.matmul(ub[:, 0:w], xa3_bf[:, t * 128:(t + 1) * 128],
                                     b3s[bufi][:, 0:w], start=False, stop=True)
                b.op("pe", flo3, inc=s_pe)
                u_done = s_pe.n
                wl = [(s_pe, g_done)]
                if itg >= 2:
                    wl.append((s_dve, h_done[itg - 2]))
                def fsi(se, sg=sg, gb=gb, w=w):
                    return se.activation(sg[:, 0:w], gb[:, 0:w], AF.Silu, bias=0.0, scale=1.0)
                b.op("act", fsi, waits=wl, inc=s_act)
                silu_done[itg] = s_act.n
                wl = [(s_pe, u_done), (s_act, silu_done[itg])]
                if itg >= 2:
                    wl.append((s_pool, stt_done[itg - 2]))
                def fh(v, th=th, sg=sg, ub=ub, w=w):
                    return v.tensor_tensor(th[:, 0:w], ub[:, 0:w], sg[:, 0:w], ALU.mult)
                b.op("dve", fh, waits=wl, inc=s_dve)
                hm_done = s_dve.n
                col = g0; rem = w; off = 0
                while rem > 0:
                    cc = col % FQ
                    run = min(rem, FQ - cc)
                    cpy = (col // FQ == 0)
                    def fsc(v, t=t, cc=cc, off=off, run=run, th=th, cpy=cpy):
                        dst = h_fold[:, t * FQ + cc: t * FQ + cc + run]
                        if cpy:
                            return v.tensor_copy(dst, th[:, off:off + run])
                        return v.tensor_tensor(dst, dst, th[:, off:off + run], ALU.add)
                    b.op("dve", fsc)
                    col += run; off += run; rem -= run
                b.op("dve", None, inc=s_dve)
                h_done[itg] = s_dve.n
                def fstt(p, th=th, w=w, bufi=bufi, t=t, fs=fs):
                    p.tensor_tensor(sdump[:, 0:w], th[:, 0:w], a2rep[bufi][:, 0:w], ALU.mult)
                    p.tensor_tensor(sdump[:, 512:512 + w], th[:, 0:w],
                                    a2rep[bufi][:, 512:512 + w], ALU.mult)
                    p.tensor_reduce(pstg[:, t * 44 + fs: t * 44 + fs + 1],
                                    sdump[:, 0:w], AX.X, ALU.add)
                    return p.tensor_reduce(pstg[:, t * 44 + 22 + fs: t * 44 + 22 + fs + 1],
                                           sdump[:, 512:512 + w], AX.X, ALU.add)
                b.op("dve", fstt, waits=[(s_dve, hm_done), (s_act, arep_ok)], inc=s_pool)
                stt_done[itg] = s_pool.n
            stripe_used[fs] = s_pe.n
            a2rep_free[fs] = s_pool.n
            if fs + 2 < 22:
                emit_stripe_load(fs + 2)
        b.barrier(bar)

    if dbg:
        for t in range(NT):
            def fdbg(sp, t=t):
                return sp.dma_start(out=hf_dbg[t * 128:(t + 1) * 128, :],
                                    in_=h_fold[:, t * FQ:(t + 1) * FQ])
            b.op("sp", fdbg, inc=sem_s, inc_amt=16)
        def fdbg2(sp):
            return sp.dma_start(out=x1f_dbg.rearrange("j p s -> p j s"),
                                in_=x1f_fm.ap().rearrange("p (j s) -> p j s", s=S))
        b.op("sp", fdbg2, inc=sem_s, inc_amt=16)
        def fdbg3(sp):
            return sp.dma_start(out=xab_dbg[0:2, :], in_=xa1_bf[:])
        b.op("sp", fdbg3, inc=sem_s, inc_amt=16)
        def fdbg3b(sp):
            return sp.dma_start(out=xab_dbg[2:4, :], in_=xa3_bf[:])
        b.op("sp", fdbg3b, inc=sem_s, inc_amt=16)
        def fdbg4(sp):
            return sp.dma_start(out=xaa_dbg2, in_=pstg[:])
        b.op("sp", fdbg4, inc=sem_s, inc_amt=16)
        for i_ in range(2):
            def fdbg5(sp, i_=i_):
                return sp.dma_start(out=arep_dbg[i_], in_=a2rep[i_][:])
            b.op("sp", fdbg5, inc=sem_s, inc_amt=16)
        b.op("sp", None, waits=[(sem_s, sem_s.n)])
        b.barrier(bar)

    if phases <= 5:
        return finish(nc, b, es, [es2, es3])

    es2.close()
    es4 = ExitStack()
    h_f_fm = es4.enter_context(nc.sbuf_tensor("h_f_fm", [128, 22 * S], BF16))
    xa2_bf = es4.enter_context(nc.sbuf_tensor("xa2_bf", [2, S], BF16))

    with ExitStack() as ph:
        tp_ps = [ph.enter_context(nc.psum_tensor(f"tp5_{i}", [128, 1024], BF16)) for i in range(2)]
        xaT_ps = ph.enter_context(nc.psum_tensor("xaT_ps", [2, 1024], BF16))
        xa2c = ph.enter_context(nc.sbuf_tensor("xa2c", [128, NT * 2], BF16))

        prev_ev = 0
        for t in range(NT):
            evs = []
            for jc in range(22):
                wdt = min(128, FQ - jc * 128)
                wv = jc // 8
                bank = tp_ps[wv % 2]
                wl = []
                if jc % 8 == 0:
                    if wv >= 2:
                        wl = [(s_act, evs[wv - 2])]
                    elif t > 0:
                        wl = [(s_act, prev_ev)]
                def ft(te, t=t, jc=jc, wdt=wdt, bank=bank):
                    return te.transpose(bank[0:wdt, (jc % 8) * 128:(jc % 8) * 128 + 128],
                                        h_fold[:, t * FQ + jc * 128: t * FQ + jc * 128 + wdt],
                                        id_sb[:])
                b.op("pe", ft, waits=wl, inc=s_pe)
                if jc % 8 == 7 or jc == 21:
                    trd = s_pe.n
                    jc0 = 8 * wv
                    nfull = min(jc - jc0 + 1, 21 - jc0 + (1 if jc < 21 else 0))
                    if jc == 21:
                        def feA(se, t=t, jc0=jc0, bank=bank):
                            dst = h_f_fm.ap().rearrange("p (jc s) -> p jc s", s=S)[
                                :, jc0:jc0 + 5, t * 128:(t + 1) * 128]
                            src = bank.ap().rearrange("p (jc c) -> p jc c", c=128)[:, 0:5, :]
                            return se.activation(dst, src, AF.Copy, bias=0.0, scale=1.0)
                        b.op("act", feA, waits=[(s_pe, trd)], inc=s_act)
                        def feB(se, t=t, bank=bank):
                            return se.activation(
                                h_f_fm[0:64, 21 * S + t * 128: 21 * S + t * 128 + 128],
                                bank[0:64, 5 * 128:6 * 128], AF.Copy, bias=0.0, scale=1.0)
                        b.op("act", feB, inc=s_act)
                    else:
                        n_in = (jc % 8) + 1
                        def fe(se, t=t, jc0=jc0, n=n_in, bank=bank):
                            dst = h_f_fm.ap().rearrange("p (jc s) -> p jc s", s=S)[
                                :, jc0:jc0 + n, t * 128:(t + 1) * 128]
                            src = bank.ap().rearrange("p (jc c) -> p jc c", c=128)[:, 0:n, :]
                            return se.activation(dst, src, AF.Copy, bias=0.0, scale=1.0)
                        b.op("act", fe, waits=[(s_pe, trd)], inc=s_act)
                    evs.append(s_act.n)
            prev_ev = s_act.n
        xa2f = ph.enter_context(nc.sbuf_tensor("xa2f", [128, NT * 2], F32))
        for t_ in range(NT):
            for r_ in range(2):
                def fcmb(v, t_=t_, r_=r_):
                    return v.tensor_reduce(xa2f[:, t_ * 2 + r_: t_ * 2 + r_ + 1],
                                           pstg[:, t_ * 44 + r_ * 22: t_ * 44 + r_ * 22 + 22],
                                           AX.X, ALU.add)
                b.op("dve", fcmb)
        b.op("dve", None, inc=s_dve)
        red_ok = s_dve.n
        def fccast(se):
            return se.activation(xa2c[:], xa2f[:], AF.Copy, bias=0.0, scale=1.0)
        b.op("act", fccast, waits=[(s_dve, red_ok)], inc=s_act)
        c_ok_act = s_act.n
        b.op("dve", None, waits=[(s_act, c_ok_act)], inc=s_dve)
        c_ok = s_dve.n
        for t in range(NT):
            wl = []
            if t == 0:
                wl = [(s_dve, c_ok)]
            elif t == 8:
                wl = [(s_act, s_act.n)]
            def ft(te, t=t):
                return te.transpose(xaT_ps[:, (t % 8) * 128:(t % 8) * 128 + 128],
                                    xa2c[:, t * 2:(t + 1) * 2], id_sb[:])
            b.op("pe", ft, waits=wl, inc=(s_pe if (t == 7 or t == 15) else None))
            if t == 7 or t == 15:
                trd = s_pe.n
                t0 = 8 * (t // 8)
                def fe(se, t0=t0):
                    return se.activation(xa2_bf[:, t0 * 128:(t0 + 8) * 128],
                                         xaT_ps[:, 0:1024], AF.Copy, bias=0.0, scale=0.5)
                b.op("act", fe, waits=[(s_pe, trd)], inc=s_act)
        b.barrier(bar)
    es3.close()

    if phases <= 6:
        return finish(nc, b, es, [es4])

    with ExitStack() as ph:
        st2 = [ph.enter_context(nc.sbuf_tensor(f"st2_{i}", [128, 22 * 512], BF16)) for i in range(2)]
        b2s = ph.enter_context(nc.sbuf_tensor("b2s", [2, D], BF16))
        fst = [ph.enter_context(nc.sbuf_tensor(f"fst{i}", [128, 512], F32)) for i in range(2)]
        x1c = [ph.enter_context(nc.sbuf_tensor(f"x1c{i}", [128, 512], BF16)) for i in range(2)]
        bnstB = ph.enter_context(nc.sbuf_tensor("bnstB", [128, NT * 48], F32))
        bnagB = ph.enter_context(nc.sbuf_tensor("bnagB", [128, 2], F32))
        stdvB = ph.enter_context(nc.sbuf_tensor("stdvB", [128, 1], F32))
        ps = [ph.enter_context(nc.psum_tensor(f"p6_{i}", [128, 512], F32)) for i in range(4)]
        s_ev = b.sem("ev6")

        def fb2(g):
            return g.dma_start(out=b2s[:], in_=b2T)
        b.op("pool", fb2, inc=sb2, inc_amt=16)
        b2_ok = sb2.n

        x1ld_ok = {}
        x1ld_free = {}
        sc6_done = {}

        def emit_x1load(it2):
            c2, t2 = it2 // NT, it2 % NT
            wl = []
            if it2 >= 2:
                wl = [(s_ev, x1ld_free[it2 - 2])]
            def fx(g, c2=c2, t2=t2, bi=it2 % 2):
                return g.dma_start(out=x1c[bi][:],
                                   in_=x1_d[t2 * 128:(t2 + 1) * 128,
                                            c2 * 512:(c2 + 1) * 512])
            b.op("pool", fx, waits=wl, inc=sx6[it2 % 2], inc_amt=16)
            x1ld_ok[it2] = sx6[it2 % 2].n
        emit_x1load(0)
        emit_x1load(1)

        def emit_ln2_scalars(tt):
            wl = []
            if tt >= 1:
                wl = [(s_act, sc6_done[tt - 1])]
            def fa(v, tt=tt):
                return v.bn_aggr(bnagB[:], bnstB[:, tt * 48:(tt + 1) * 48])
            b.op("dve", fa, waits=wl, inc=s_dve)
            ag_ok = s_dve.n
            def f4(se):
                return se.activation(stdvB[:], bnagB[:, 1:2], AF.Ln,
                                     bias=eps_sb[:, 0:1], scale=1.0)
            b.op("act", f4, waits=[(s_dve, ag_ok)])
            def f4b(se, tt=tt):
                return se.activation(r2_all[:, tt:tt + 1], stdvB[:], AF.Exp,
                                     bias=0.0, scale=-0.5)
            b.op("act", f4b)
            def f4c(se, tt=tt):
                return se.activation(m2_all[:, tt:tt + 1], r2_all[:, tt:tt + 1],
                                     AF.Copy, bias=0.0, scale=bnagB[:, 0:1])
            b.op("act", f4c, inc=s_act)
            sc6_done[tt] = s_act.n

        ld_vals = {}
        s2_used = {}
        def emit_s2(c):
            bufi = c % 2
            waits = []
            if c >= 2:
                waits = [(s_pe, s2_used[c - 2])]
            def fA(g, c=c, bufi=bufi):
                src = w2qT[0:2688, c * 512:(c + 1) * 512].rearrange("(jc p) c -> p jc c", p=128)
                dst = st2[bufi].ap().rearrange("p (jc c) -> p jc c", c=512)[:, 0:21, :]
                return g.dma_start(out=dst, in_=src)
            b.op("pool", fA, waits=waits, inc=s6w[c % 2], inc_amt=16)
            def fB(g, c=c, bufi=bufi):
                return g.dma_start(out=st2[bufi][0:64, 21 * 512:22 * 512],
                                   in_=w2qT[2688:2752, c * 512:(c + 1) * 512])
            b.op("pool", fB, inc=s6w[c % 2], inc_amt=16)
            ld_vals[c] = s6w[c % 2].n
        emit_s2(0); emit_s2(1)

        last_store6 = {0: sst[0].n, 1: sst[1].n}
        it = 0
        for c in range(8):
            bufi = c % 2
            for t in range(NT):
                bank = ps[it % 4]
                wl = [(s6w[c % 2], ld_vals[c]), (sb2, b2_ok)]
                if it >= 4:
                    wl.append((s_ev, it - 3))
                for jc in range(22):
                    kw = min(128, FQ - jc * 128)
                    def f(te, jc=jc, t=t, kw=kw, bank=bank, bufi=bufi):
                        return te.matmul(bank[:],
                                         h_f_fm[0:kw, jc * S + t * 128: jc * S + t * 128 + 128],
                                         st2[bufi][0:kw, jc * 512:(jc + 1) * 512],
                                         start=(jc == 0), stop=False)
                    b.op("pe", f, waits=(wl if jc == 0 else []))
                def flo(te, t=t, c=c, bank=bank):
                    return te.matmul(bank[:], xa2_bf[:, t * 128:(t + 1) * 128],
                                     b2s[:, c * 512:(c + 1) * 512], start=False, stop=True)
                b.op("pe", flo, inc=s_pe)
                mm_done = s_pe.n
                fb = fst[it % 2]
                w2_ = [(s_pe, mm_done), (sst[it % 2], last_store6[it % 2]),
                       (sx6[it % 2], x1ld_ok[it])]
                def fy(v, bank=bank, fb=fb, bi=it % 2):
                    return v.tensor_tensor(fb[:], bank[:], x1c[bi][:], ALU.add)
                b.op("dve", fy, waits=w2_, inc=s_ev)
                ev_ok = s_ev.n
                x1ld_free[it] = ev_ok
                def fst_(v, fb=fb, t=t, c=c):
                    return v.bn_stats(bnstB[:, t * 48 + c * 6: t * 48 + c * 6 + 6], fb[:])
                b.op("dve", fst_)
                if it + 2 < 128:
                    emit_x1load(it + 2)
                def f2(sp, fb=fb, t=t, c=c):
                    return sp.dma_start(out=f_d[t * 128:(t + 1) * 128, c * 512:(c + 1) * 512],
                                        in_=fb[:])
                b.op("sp", f2, waits=[(s_ev, ev_ok)], inc=sst[it % 2], inc_amt=16)
                last_store6[it % 2] = sst[it % 2].n
                if c == 7 and t >= 1:
                    emit_ln2_scalars(t - 1)
                it += 1
            s2_used[c] = s_pe.n
            if c + 2 < 8:
                emit_s2(c + 2)
        def fsp6(v):
            return v.tensor_copy(x1c[0][:], x1c[1][:])
        b.op("dve", fsp6)
        emit_ln2_scalars(NT - 1)
        b.op("sp", None, waits=[(sst[0], sst[0].n), (sst[1], sst[1].n)])
        b.barrier(bar)
    es4.close()

    with ExitStack() as ph:
        f_sb = [ph.enter_context(nc.sbuf_tensor(f"f_sb{i}", [128, D], F32)) for i in range(2)]
        o_sb = [ph.enter_context(nc.sbuf_tensor(f"o_sb{i}", [128, D], F32)) for i in range(2)]

        o_done = {}
        o_store = {}
        for t in range(NT):
            fb, ob = f_sb[t % 2], o_sb[t % 2]
            wl = []
            if t >= 2:
                wl = [(s_dve, o_done[t - 2]), (so6[t % 2], o_store[t - 2])]
            def fl1(sp, fb=fb, t=t):
                return sp.dma_start(out=fb[:], in_=f_d[t * 128:(t + 1) * 128, :])
            b.op("sp", fl1, waits=wl, inc=sld6[t % 2], inc_amt=16)
            ld_ok = sld6[t % 2].n
            def f6(v, ob=ob, fb=fb, t=t):
                return v.tensor_scalar(ob[:], fb[:], r2_all[:, t:t + 1],
                                       m2_all[:, t:t + 1],
                                       ALU.mult, ALU.subtract)
            b.op("dve", f6, waits=[(sld6[t % 2], ld_ok)], inc=s_dve)
            o_done[t] = s_dve.n
            def f7(sp, ob=ob, t=t):
                return sp.dma_start(out=y_out[t * 128:(t + 1) * 128, :], in_=ob[:])
            b.op("sp", f7, waits=[(s_dve, o_done[t])], inc=so6[t % 2], inc_amt=16)
            o_store[t] = so6[t % 2].n
        b.op("sp", None, waits=[(so6[0], so6[0].n), (so6[1], so6[1].n)])

    return finish(nc, b, es, [])


def finish(nc, b, es, extra_stacks):
    for st in extra_stacks:
        try:
            st.close()
        except Exception:
            pass
    with ExitStack() as run:
        for s in b.sems:
            s.h = run.enter_context(nc.semaphore(s.name))
        block = run.enter_context(nc.Block())
        block.tensor(lambda e: b.replay("pe", e))
        block.scalar(lambda e: b.replay("act", e))
        block.vector(lambda e: b.replay("dve", e))
        block.gpsimd(lambda e: b.replay("pool", e))
        block.sync(lambda e: b.replay("sp", e))
    es.close()
    return nc


def _consts():
    i = np.arange(64, dtype=np.float64)
    invf = THETA ** (-2.0 * i / HD)
    t = np.arange(S, dtype=np.float64)
    ang = t[:, None] * invf[None, :]
    cos = np.cos(ang); sin = np.sin(ang)
    sc = 1.0 / np.sqrt(HD)
    ck = np.concatenate([cos, cos], axis=1)
    sk = np.concatenate([-sin, sin], axis=1)
    rkc_ = np.tile(ck, (1, 4)).astype(ml_dtypes.bfloat16)
    rks_ = np.tile(sk, (1, 4)).astype(ml_dtypes.bfloat16)
    rqct_ = (np.tile(ck, (1, 4)) * sc).astype(ml_dtypes.bfloat16)
    rqst_ = (np.tile(sk, (1, 4)) * sc).astype(ml_dtypes.bfloat16)
    ident = np.eye(128, dtype=ml_dtypes.bfloat16)
    mask = np.triu(np.full((128, 128), -1e30, dtype=np.float32), k=1)
    return rqct_, rqst_, rkc_, rks_, ident, mask


def make_in_maps(L0_attn_x, L0_attn_Wq, L0_attn_Wk, L0_attn_Wv, L0_attn_Wo,
                 L0_ffn_w1q, L0_ffn_a1, L0_ffn_b1, L0_ffn_w3q, L0_ffn_a3,
                 L0_ffn_b3, L0_ffn_w2q, L0_ffn_a2, L0_ffn_b2):
    x = np.ascontiguousarray(np.asarray(L0_attn_x, dtype=np.float32))
    rqct_, rqst_, rkc_, rks_, ident, mask = _consts()
    shared = {
        "Wq": np.ascontiguousarray(np.asarray(L0_attn_Wq, dtype=np.float32)),
        "Wk": np.ascontiguousarray(np.asarray(L0_attn_Wk, dtype=np.float32)),
        "Wv": np.ascontiguousarray(np.asarray(L0_attn_Wv, dtype=np.float32)),
        "Wo": np.ascontiguousarray(np.asarray(L0_attn_Wo, dtype=np.float32)),
        "w1qT": np.ascontiguousarray(np.asarray(L0_ffn_w1q, dtype=np.float32).T),
        "w3qT": np.ascontiguousarray(np.asarray(L0_ffn_w3q, dtype=np.float32).T),
        "w2qT": np.ascontiguousarray(np.asarray(L0_ffn_w2q, dtype=np.float32).T),
        "a1": np.ascontiguousarray(np.asarray(L0_ffn_a1, dtype=np.float32)),
        "a3": np.ascontiguousarray(np.asarray(L0_ffn_a3, dtype=np.float32)),
        "a2": np.ascontiguousarray(np.asarray(L0_ffn_a2, dtype=np.float32)),
        "b1T": np.ascontiguousarray(np.asarray(L0_ffn_b1, dtype=np.float32).T),
        "b3T": np.ascontiguousarray(np.asarray(L0_ffn_b3, dtype=np.float32).T),
        "b2T": np.ascontiguousarray(np.asarray(L0_ffn_b2, dtype=np.float32).T),
        "identc": ident,
        "maskb": np.ascontiguousarray(mask.T).astype(ml_dtypes.bfloat16),
        "rqct": rqct_, "rqst": rqst_, "rkc": rkc_, "rks": rks_,
    }
    in_maps = []
    for core in range(8):
        m = dict(shared)
        xb = np.ascontiguousarray(x[core])
        m["x"] = xb
        m["xT"] = np.ascontiguousarray(xb.T)
        in_maps.append(m)
    return in_maps


def kernel(**inputs):
    in_maps = make_in_maps(**inputs)
    nc = build_program()
    res = run_bass_kernel_spmd(nc, in_maps, core_ids=list(range(8)))
    return np.stack([np.asarray(r["y"], dtype=np.float32) for r in res.results], axis=0)


# version 39
# speedup vs baseline: 1.0104x; 1.0104x over previous
import numpy as np
import ml_dtypes
from contextlib import ExitStack

import concourse.bass as bass
import concourse.mybir as mybir
from concourse.bass_utils import run_bass_kernel_spmd

F32 = mybir.dt.float32
BF16 = mybir.dt.bfloat16
AF = mybir.ActivationFunctionType
ALU = mybir.AluOpType
AX = mybir.AxisListType

S = 2048
D = 4096
HQ, HKV, HD = 32, 8, 128
FF = 11008
FQ = 2752
DKV = HKV * HD
NKC = D // 128
NT = S // 128
NG = S // 512
THETA = 10000.0
EPS = 1e-5

ENGINES = ("pe", "act", "dve", "pool", "sp")


class Sem:
    __slots__ = ("name", "n", "h")
    def __init__(self, name):
        self.name = name
        self.n = 0
        self.h = None


class Builder:
    def __init__(self):
        self.ops = {k: [] for k in ENGINES}
        self.sems = []

    def sem(self, name):
        s = Sem(name)
        self.sems.append(s)
        return s

    def op(self, engine, fn=None, waits=(), inc=None, inc_amt=1):
        w = [(s, v) for (s, v) in waits if v > 0]
        if inc is not None:
            inc.n += inc_amt
        self.ops[engine].append((fn, w, inc, inc_amt))
        return inc.n if inc is not None else None

    def barrier(self, bar):
        target = bar.n + 5
        bar.n = target
        for e in ENGINES:
            self.ops[e].append((None, [], bar, 1))
            self.ops[e].append((None, [(bar, target)], None, 1))

    def replay(self, engine, eng):
        for fn, waits, inc, amt in self.ops[engine]:
            for s, v in waits:
                eng.wait_ge(s.h, v)
            ins = fn(eng) if fn is not None else None
            if inc is not None:
                if ins is None:
                    eng.sem_inc(inc.h, amt)
                else:
                    ins.then_inc(inc.h, amt)


def build_program(dbg=False, phases=99):
    nc = bass.Bass("TRN2", target_bir_lowering=False)
    b = Builder()

    def din(name, shape, dt=F32):
        return nc.dram_tensor(name, list(shape), dt, kind="ExternalInput").ap()

    x_in = din("x", (S, D))
    xT_in = din("xT", (D, S))
    Wq = din("Wq", (D, D)); Wk = din("Wk", (D, DKV)); Wv = din("Wv", (D, DKV))
    Wo = din("Wo", (D, D))
    w1qT = din("w1qT", (D // 4, FF)); w3qT = din("w3qT", (D // 4, FF))
    w2qT = din("w2qT", (FQ, D))
    a1 = din("a1", (2, D)); a3 = din("a3", (2, D)); a2 = din("a2", (2, FF))
    b1T = din("b1T", (2, FF)); b3T = din("b3T", (2, FF)); b2T = din("b2T", (2, D))
    identc = din("identc", (128, 128), BF16)
    maskb = din("maskb", (128, 128), BF16)
    rkc = din("rkc", (S, 512), BF16)
    rks = din("rks", (S, 512), BF16)
    rqct = din("rqct", (S, 512), BF16)
    rqst = din("rqst", (S, 512), BF16)

    y_out = nc.dram_tensor("y", [S, D], F32, kind="ExternalOutput").ap()

    dkind = "ExternalOutput" if dbg else "Internal"
    q_d = nc.dram_tensor("q_d", [S, D], BF16, kind=dkind).ap()
    k_d = nc.dram_tensor("k_d", [S, DKV], BF16, kind=dkind).ap()
    v_d = nc.dram_tensor("v_d", [HKV, S, HD], BF16, kind=dkind).ap()
    at_d = nc.dram_tensor("at_d", [S, D], BF16, kind=dkind).ap()
    wo_d = nc.dram_tensor("wo_d", [S, D], F32, kind=dkind).ap()
    x1_d = nc.dram_tensor("x1_d", [S, D], BF16, kind=dkind).ap()
    f_d = nc.dram_tensor("f_d", [S, D], F32, kind=dkind).ap()
    hf_dbg = nc.dram_tensor("hf_dbg", [S, FQ], BF16, kind=dkind).ap()
    x1f_dbg = nc.dram_tensor("x1f_dbg", [8, 128, S], BF16, kind=dkind).ap()
    xab_dbg = nc.dram_tensor("xab_dbg", [4, S], BF16, kind=dkind).ap()
    xaa_dbg2 = nc.dram_tensor("xaa_dbg2", [128, NT * 44], F32, kind=dkind).ap()
    arep_dbg = nc.dram_tensor("arep_dbg", [2, 128, 1024], BF16, kind=dkind).ap()

    sem_s = b.sem("s")
    s_pe = b.sem("pe")
    s_act = b.sem("act")
    s_dve = b.sem("dve")
    s_pool = b.sem("pool")
    bar = b.sem("bar")
    s_const = b.sem("cst")
    s_xfm = b.sem("xfm")
    s_xg = [b.sem(f"xg{i}") for i in range(4)]
    swb = [b.sem(f"swb{i}") for i in range(3)]
    sst = [b.sem(f"sst{i}") for i in range(2)]
    stab = [b.sem(f"stab{i}") for i in range(2)]
    sqr = [b.sem(f"sqr{i}") for i in range(2)]
    skv = [b.sem(f"skv{i}") for i in range(2)]
    sld3 = [b.sem(f"sld3{i}") for i in range(2)]
    sx1s = [b.sem(f"sx1s{i}") for i in range(2)]
    s4w = [b.sem(f"s4w{i}") for i in range(2)]
    s6w = [b.sem(f"s6w{i}") for i in range(2)]
    sb2 = b.sem("sb2")
    sld6 = [b.sem(f"sld6{i}") for i in range(2)]
    so6 = [b.sem(f"so6{i}") for i in range(2)]
    sa13 = b.sem("sa13")
    sx3 = [b.sem(f"sx3_{i}") for i in range(2)]
    sx6 = [b.sem(f"sx6_{i}") for i in range(2)]

    es = ExitStack()

    id_sb = es.enter_context(nc.sbuf_tensor("id_sb", [128, 128], BF16))
    maskb_sb = es.enter_context(nc.sbuf_tensor("maskb_sb", [128, 128], BF16))

    for dst, src in ((id_sb, identc), (maskb_sb, maskb)):
        def f(g, dst=dst, src=src):
            return g.dma_start(out=dst[:], in_=src)
        b.op("pool", f, inc=s_const, inc_amt=16)
    for e_ in ("pe", "dve", "pool"):
        b.op(e_, None, waits=[(s_const, 32)])
    eps_sb = es.enter_context(nc.sbuf_tensor("eps_sb", [128, 1], F32))
    def f_eps(v):
        return v.memset(eps_sb[:], EPS)
    b.op("dve", f_eps, inc=s_dve)
    r1_all = es.enter_context(nc.sbuf_tensor("r1_all", [128, NT], F32))
    m1_all = es.enter_context(nc.sbuf_tensor("m1_all", [128, NT], F32))
    r2_all = es.enter_context(nc.sbuf_tensor("r2_all", [128, NT], F32))
    m2_all = es.enter_context(nc.sbuf_tensor("m2_all", [128, NT], F32))

    with ExitStack() as ph:
        x_fm = ph.enter_context(nc.sbuf_tensor("x_fm", [128, NKC * S], BF16))
        stb = [ph.enter_context(nc.sbuf_tensor(f"stb{i}", [128, 16 * 512], BF16))
               for i in range(3)]
        qst = [ph.enter_context(nc.sbuf_tensor(f"qst{i}", [128, 512], BF16))
               for i in range(2)]
        ktc = [ph.enter_context(nc.sbuf_tensor(f"ktc{i}", [128, 512], BF16))
               for i in range(2)]
        kts = [ph.enter_context(nc.sbuf_tensor(f"kts{i}", [128, 512], BF16))
               for i in range(2)]
        ksw = ph.enter_context(nc.sbuf_tensor("ksw", [128, 512], F32))
        kt1 = ph.enter_context(nc.sbuf_tensor("kt1", [128, 512], BF16))
        ps = [ph.enter_context(nc.psum_tensor(f"p1_{i}", [128, 512], F32))
              for i in range(4)]

        def emit_half(j):
            w_ap, kind, c = stripes[j // 2]
            h = j % 2
            buf = stb[j % 3]
            waits = []
            if j - 3 in buf_free:
                waits.append(buf_free[j - 3])
            def f(g, w_ap=w_ap, c=c, h=h, buf=buf):
                src = w_ap[h * 2048:(h + 1) * 2048, c * 512:(c + 1) * 512] \
                    .rearrange("(kc p) c -> p kc c", p=128)
                dst = buf.ap().rearrange("p (kc c) -> p kc c", c=512)
                return g.dma_start(out=dst, in_=src)
            b.op("pool", f, waits=waits, inc=swb[j % 3], inc_amt=16)
            halves_loaded[j] = swb[j % 3].n

        stripes = [(Wq, 'q', c) for c in range(8)] + \
                  [(Wk, 'k', c) for c in range(2)] + \
                  [(Wv, 'v', c) for c in range(2)]

        halves_loaded = {}
        buf_free = {}
        for kc in range(NKC):
            def f(g, kc=kc):
                return g.dma_start(out=x_fm[:, kc * S:(kc + 1) * S],
                                   in_=xT_in[kc * 128:(kc + 1) * 128, :])
            b.op("pool", f, inc=s_xg[kc // 8], inc_amt=16)
            if kc == 3:
                emit_half(0); emit_half(1); emit_half(2)
        gdone = [s_xg[g].n for g in range(4)]
        xfm_all = [(s_xg[g], gdone[g]) for g in range(4)]

        s_ev = b.sem("ev1")
        last_store = {0: 0, 1: 0}
        ktab_free = {}
        kti = 0


        it = 0
        for si, (w_ap, kind, c) in enumerate(stripes):
            for t in range(NT):
                bank = ps[it % 4]
                fine = si <= 3
                waits = ([(s_xg[0], gdone[0])] if fine else list(xfm_all))
                waits += [(swb[(2 * si) % 3], halves_loaded[2 * si]),
                          (swb[(2 * si + 1) % 3], halves_loaded[2 * si + 1])]
                if it >= 4:
                    waits.append((s_ev, it - 3))
                for kc in range(NKC):
                    buf = stb[(2 * si + kc // 16) % 3]
                    def f(te, buf=buf, kc=kc, t=t, bank=bank):
                        return te.matmul(bank[:],
                                         x_fm[:, kc * S + t * 128: kc * S + t * 128 + 128],
                                         buf[:, (kc % 16) * 512:(kc % 16 + 1) * 512],
                                         start=(kc == 0), stop=(kc == NKC - 1))
                    ww = [(s_xg[kc // 8], gdone[kc // 8])] \
                        if (fine and kc % 8 == 0 and kc > 0) else []
                    if kc == 0:
                        b.op("pe", f, waits=waits)
                    elif kc == 15 and t == NT - 1:
                        b.op("pe", f, waits=ww, inc=s_pe)
                        buf_free[2 * si] = (s_pe, s_pe.n)
                    elif kc == NKC - 1:
                        b.op("pe", f, waits=ww, inc=s_pe)
                        if t == NT - 1:
                            buf_free[2 * si + 1] = (s_pe, s_pe.n)
                    else:
                        b.op("pe", f, waits=ww)
                mm_done = s_pe.n

                if kind == 'v':
                    qb = qst[it % 2]
                    w2_ = [(s_pe, mm_done)]
                    if last_store[it % 2] > 0:
                        w2_.append((sst[it % 2], last_store[it % 2]))
                    def f(v, bank=bank, qb=qb):
                        return v.tensor_copy(qb[:], bank[:])
                    b.op("dve", f, waits=w2_, inc=s_ev)
                    ev_ok = s_ev.n
                    def f2(sp, qb=qb, t=t, c=c):
                        dst = v_d[c * 4:(c + 1) * 4, t * 128:(t + 1) * 128, :] \
                            .rearrange("h t d -> t h d")
                        src = qb.ap().rearrange("p (h d) -> p h d", d=128)
                        return sp.dma_start(out=dst, in_=src)
                    b.op("sp", f2, waits=[(s_ev, ev_ok)], inc=sst[it % 2], inc_amt=16)
                    last_store[it % 2] = sst[it % 2].n
                else:
                    tabc, tabs = (rqct, rqst) if kind == 'q' else (rkc, rks)
                    tbi = kti % 2
                    tc, ts_ = ktc[tbi], kts[tbi]
                    wl = []
                    if tbi in ktab_free:
                        wl = [(s_dve, ktab_free[tbi])]
                    def fl1(sp, tc=tc, t=t, tabc=tabc):
                        return sp.dma_start(out=tc[:], in_=tabc[t * 128:(t + 1) * 128, :])
                    b.op("sp", fl1, waits=wl, inc=stab[tbi], inc_amt=16)
                    def fl2(sp, ts_=ts_, t=t, tabs=tabs):
                        return sp.dma_start(out=ts_[:], in_=tabs[t * 128:(t + 1) * 128, :])
                    b.op("sp", fl2, inc=stab[tbi], inc_amt=16)
                    tab_ok = stab[tbi].n
                    qb = qst[it % 2]
                    w2_ = [(s_pe, mm_done), (stab[tbi], tab_ok)]
                    if last_store[it % 2] > 0:
                        w2_.append((sst[it % 2], last_store[it % 2]))
                    def f1(v, bank=bank):
                        return v.tensor_copy(
                            ksw.ap().rearrange("p (h q i) -> p h q i", h=4, q=2, i=64)[:, :, 0:1, :],
                            bank.ap().rearrange("p (h q i) -> p h q i", h=4, q=2, i=64)[:, :, 1:2, :])
                    b.op("dve", f1, waits=w2_)
                    def f2(v, bank=bank):
                        return v.tensor_copy(
                            ksw.ap().rearrange("p (h q i) -> p h q i", h=4, q=2, i=64)[:, :, 1:2, :],
                            bank.ap().rearrange("p (h q i) -> p h q i", h=4, q=2, i=64)[:, :, 0:1, :])
                    b.op("dve", f2, inc=s_dve)
                    sw_ok = s_dve.n
                    def f3(v, bank=bank, tc=tc):
                        return v.tensor_tensor(kt1[:], bank[:], tc[:], ALU.mult)
                    b.op("dve", f3, inc=s_ev)
                    kt_ok = s_ev.n
                    def f4(v, ts_=ts_, qb=qb):
                        return v.tensor_tensor(qb[:], ksw[:], ts_[:], ALU.mult)
                    b.op("dve", f4, waits=[(s_dve, sw_ok)], inc=s_dve)
                    qb_ok = s_dve.n
                    def f5(v, qb=qb):
                        return v.tensor_tensor(qb[:], qb[:], kt1[:], ALU.add)
                    b.op("dve", f5, waits=[(s_dve, qb_ok), (s_ev, kt_ok)], inc=s_dve)
                    ktab_free[tbi] = s_dve.n
                    rope_done = s_dve.n
                    dst_d = q_d if kind == 'q' else k_d
                    def f6(sp, qb=qb, t=t, c=c, dst_d=dst_d):
                        return sp.dma_start(
                            out=dst_d[t * 128:(t + 1) * 128, c * 512:(c + 1) * 512],
                            in_=qb[:])
                    b.op("sp", f6, waits=[(s_dve, rope_done)], inc=sst[it % 2], inc_amt=16)
                    last_store[it % 2] = sst[it % 2].n
                    kti += 1
                it += 1
            nj = 2 * si + 3
            if nj < 2 * len(stripes):
                emit_half(nj)
            nj = 2 * si + 4
            if nj < 2 * len(stripes):
                emit_half(nj)

        b.op("sp", None, waits=[(sst[0], sst[0].n), (sst[1], sst[1].n)])
        b.barrier(bar)

    if phases <= 1:
        return finish(nc, b, es, [])

    with ExitStack() as ph:
        qfm = [ph.enter_context(nc.sbuf_tensor(f"q_fm{i}", [128, HQ * 512], BF16))
               for i in range(2)]
        ksl = [ph.enter_context(nc.sbuf_tensor(f"ksl{i}", [128, S], BF16)) for i in range(2)]
        vsl = [ph.enter_context(nc.sbuf_tensor(f"vsl{i}", [128, S], BF16)) for i in range(2)]
        p_sb = [ph.enter_context(nc.sbuf_tensor(f"p_sb{i}", [128, S], BF16)) for i in range(3)]
        recs = [ph.enter_context(nc.sbuf_tensor(f"recs{i}", [128, 1], F32)) for i in range(4)]
        attn_g = ph.enter_context(nc.sbuf_tensor("attn_g", [128, 4 * D], BF16))
        ones1 = ph.enter_context(nc.sbuf_tensor("ones1", [128, 1], BF16))
        sps2 = ph.enter_context(nc.psum_tensor("sps2", [128, 2048], F32))
        pv_ps = [ph.enter_context(nc.psum_tensor(f"pv_ps{i}", [128, 512], F32))
                 for i in range(2)]
        ssum_ps = [ph.enter_context(nc.psum_tensor(f"ssum_ps{i}", [128, 4], F32))
                   for i in range(2)]

        def fones(v):
            return v.memset(ones1[:], 1.0)
        b.op("dve", fones, inc=s_dve)
        ones_ok = s_dve.n

        nu = 0
        half_exp = {0: 0, 1: 0}
        hp = 0
        kbuf_free = {0: 0, 1: 0}
        att_store_done = 0
        q_ready_h = {}
        pe_end = {-1: 0, -2: 0}

        def emit_qprep(g):
            qf = qfm[g % 2]
            for h in range(HQ):
                wl = []
                if g >= 2 and h == 0:
                    wl = [(s_pe, pe_end[g - 2])]
                def fl(sp, qf=qf, h=h, g=g):
                    return sp.dma_start_transpose(
                        out=qf[:, h * 512:(h + 1) * 512],
                        in_=q_d[g * 512:(g + 1) * 512, h * 128:(h + 1) * 128])
                b.op("sp", fl, waits=wl, inc=sqr[g % 2], inc_amt=16)
            q_ready_h[g] = sqr[g % 2].n

        emit_qprep(0)
        fo_done = {}
        pv_done_d = {}
        rec_done_d = {}
        for g in range(NG):
            q_fm = qfm[g % 2]
            pend1 = None
            pend2 = None

            def emit_pv(u):
                n_ = u["nu"]; ui = u["ui"]; nkb = u["nkb"]; vsb = u["vsb"]
                pvb = pv_ps[n_ % 2]
                sl = (n_ // 2) % 4
                ssb = ssum_ps[n_ % 2]
                sc_ = (n_ // 2) % 4
                for kb in range(nkb):
                    w_ = []
                    if kb == 0:
                        wv = u["expd"]
                        if (n_ - 2) in fo_done:
                            wv = max(wv, fo_done[n_ - 2])
                        w_ = [(s_act, wv)]
                        w_.append((s_dve, rec_done_d[n_ - 2] if (n_ - 2) in rec_done_d
                                   else ones_ok))
                    def fp(te, kb=kb, ui=ui, vsb=vsb, nkb=nkb, pvb=pvb, sl=sl):
                        return te.matmul(pvb[:, sl * 128:sl * 128 + 128],
                                         p_sb[ui][:, kb * 128:(kb + 1) * 128],
                                         vsb[:, kb * 128:(kb + 1) * 128],
                                         start=(kb == 0), stop=(kb == nkb - 1))
                    b.op("pe", fp, waits=w_)
                    def fs(te, kb=kb, ui=ui, nkb=nkb, ssb=ssb, sc_=sc_):
                        return te.matmul(ssb[:, sc_:sc_ + 1],
                                         p_sb[ui][:, kb * 128:(kb + 1) * 128],
                                         ones1[:],
                                         start=(kb == 0), stop=(kb == nkb - 1))
                    b.op("pe", fs, inc=(s_pe if kb == nkb - 1 else None))
                pv_done_d[n_] = s_pe.n
                def fc(v, n_=n_, ssb=ssb, sc_=sc_):
                    return v.reciprocal(recs[n_ % 4][:], ssb[:, sc_:sc_ + 1])
                b.op("dve", fc, waits=[(s_pe, pv_done_d[n_])], inc=s_dve)
                rec_done_d[n_] = s_dve.n
                w_ = [(s_pe, pv_done_d[n_]), (s_dve, rec_done_d[n_])]
                if u["store_wait"] > 0:
                    w_.append((sem_s, u["store_wait"]))
                def fo(se, qs=u["qs"], h=u["h"], n_=n_, pvb=pvb, sl=sl):
                    return se.activation(
                        attn_g[:, qs * D + h * 128: qs * D + h * 128 + 128],
                        pvb[:, sl * 128:sl * 128 + 128], AF.Copy, bias=0.0,
                        scale=recs[n_ % 4][:, 0:1])
                b.op("act", fo, waits=w_, inc=s_act)
                fo_done[n_] = s_act.n

            for kv in range(HKV):
                ksb, vsb = ksl[kv % 2], vsl[kv % 2]
                wl = []
                if kbuf_free[kv % 2] > 0:
                    wl = [(s_pe, kbuf_free[kv % 2])]
                def fk(sp, ksb=ksb, g=g, kv=kv):
                    return sp.dma_start_transpose(
                        out=ksb[:, 0:(g + 1) * 512],
                        in_=k_d[0:(g + 1) * 512, kv * 128:(kv + 1) * 128])
                b.op("sp", fk, waits=wl, inc=skv[kv % 2], inc_amt=16)
                def fv(sp, vsb=vsb, g=g, kv=kv):
                    src = v_d[kv:kv + 1, 0:(g + 1) * 512, :] \
                        .rearrange("o (kb p) d -> p (o kb) d", p=128)
                    dst = vsb.ap().rearrange("p (kb d) -> p kb d", d=128)[:, 0:(g + 1) * 4, :]
                    return sp.dma_start(out=dst, in_=src)
                b.op("sp", fv, inc=skv[kv % 2], inc_amt=16)
                kv_ok = skv[kv % 2].n
                if kv == 1 and g + 1 < NG:
                    emit_qprep(g + 1)

                for hh in range(4):
                    h = kv * 4 + hh
                    for qs in range(4):
                        qt = g * 4 + qs
                        klen = (qt + 1) * 128
                        nkb = qt + 1
                        ui = nu % 3
                        if klen <= 1024:
                            off0 = hp * 1024
                            occ = [hp]
                            hp ^= 1
                        else:
                            off0 = 0
                            occ = [0, 1]
                        for kb in range(nkb):
                            w_ = []
                            if kb == 0:
                                w_ = [(skv[kv % 2], kv_ok),
                                      (sqr[g % 2], q_ready_h[g])]
                                for hh_ in occ:
                                    if half_exp[hh_] > 0:
                                        w_.append((s_act, half_exp[hh_]))
                            last_chunk = (kb == nkb - 1)
                            def f(te, kb=kb, h=h, qs=qs, ksb=ksb,
                                  q_fm=q_fm, off0=off0, lc=last_chunk):
                                return te.matmul(
                                    sps2[:, off0 + kb * 128:off0 + kb * 128 + 128],
                                    ksb[:, kb * 128:(kb + 1) * 128],
                                    q_fm[:, h * 512 + qs * 128: h * 512 + qs * 128 + 128],
                                    start=True, stop=(not lc))
                            b.op("pe", f, waits=w_)
                        def fmm(te, off0=off0, qt=qt):
                            return te.matmul(
                                sps2[:, off0 + qt * 128:off0 + qt * 128 + 128],
                                id_sb[:], maskb_sb[:], start=False, stop=True)
                        b.op("pe", fmm, inc=s_pe)
                        s_done = s_pe.n
                        def fe(se, off0=off0, klen=klen, ui=ui):
                            return se.activation(p_sb[ui][:, 0:klen],
                                                 sps2[:, off0:off0 + klen],
                                                 AF.Exp, bias=0.0, scale=1.0)
                        b.op("act", fe, waits=[(s_pe, s_done)], inc=s_act)
                        for hh_ in occ:
                            half_exp[hh_] = s_act.n
                        u_cur = dict(nu=nu, ui=ui, nkb=nkb, vsb=vsb, qs=qs, h=h,
                                     expd=s_act.n,
                                     store_wait=(att_store_done if
                                                 (g > 0 and nu % (HKV * 16) == 0) else 0),
                                     kv=kv, g=g)
                        if pend2 is not None:
                            emit_pv(pend2)
                            if pend2["kv"] != pend1["kv"]:
                                kbuf_free[pend2["kv"] % 2] = s_pe.n
                        pend2 = pend1
                        pend1 = u_cur
                        nu += 1
            if pend2 is not None:
                emit_pv(pend2)
                if pend2["kv"] != pend1["kv"]:
                    kbuf_free[pend2["kv"] % 2] = s_pe.n
                pend2 = None
            if pend1 is not None:
                emit_pv(pend1)
                kbuf_free[pend1["kv"] % 2] = s_pe.n
                pend1 = None
            pe_end[g] = s_pe.n
            att_ev = s_act.n
            for qs in range(4):
                def fs_(sp, g=g, qs=qs):
                    return sp.dma_start(
                        out=at_d[(g * 4 + qs) * 128:(g * 4 + qs + 1) * 128, :],
                        in_=attn_g[:, qs * D:(qs + 1) * D])
                b.op("sp", fs_, waits=[(s_act, att_ev)], inc=sem_s, inc_amt=16)
            att_store_done = sem_s.n
        b.op("sp", None, waits=[(sem_s, sem_s.n)])
        b.barrier(bar)

    if phases <= 2:
        return finish(nc, b, es, [])

    with ExitStack() as ph:
        a_fm = ph.enter_context(nc.sbuf_tensor("a_fm", [128, NKC * S], BF16))
        stb = [ph.enter_context(nc.sbuf_tensor(f"stb3{i}", [128, 16 * 512], BF16))
               for i in range(3)]
        ost = [ph.enter_context(nc.sbuf_tensor(f"ost{i}", [128, 512], F32)) for i in range(2)]
        xr3 = [ph.enter_context(nc.sbuf_tensor(f"xr3_{i}", [128, 512], F32)) for i in range(2)]
        bnstA = ph.enter_context(nc.sbuf_tensor("bnstA", [128, NT * 48], F32))
        bnagA = ph.enter_context(nc.sbuf_tensor("bnagA", [128, 2], F32))
        stdvA = ph.enter_context(nc.sbuf_tensor("stdvA", [128, 1], F32))
        ps = [ph.enter_context(nc.psum_tensor(f"p3_{i}", [128, 512], F32)) for i in range(4)]
        s_ev = b.sem("ev3")

        def emit_xload(it2):
            c2, t2 = it2 // NT, it2 % NT
            wl = []
            if it2 >= 2:
                wl = [(s_ev, xld_free[it2 - 2])]
            def fx(g, c2=c2, t2=t2, bi=it2 % 2):
                return g.dma_start(out=xr3[bi][:],
                                   in_=x_in[t2 * 128:(t2 + 1) * 128,
                                            c2 * 512:(c2 + 1) * 512])
            b.op("pool", fx, waits=wl, inc=sx3[it2 % 2], inc_amt=16)
            xld_ok[it2] = sx3[it2 % 2].n

        xld_ok = {}
        xld_free = {}
        sc_done = {}
        emit_xload(0)
        emit_xload(1)

        def emit_ln1_scalars(tt):
            wl = []
            if tt >= 1:
                wl = [(s_act, sc_done[tt - 1])]
            def fa(v, tt=tt):
                return v.bn_aggr(bnagA[:], bnstA[:, tt * 48:(tt + 1) * 48])
            b.op("dve", fa, waits=wl, inc=s_dve)
            ag_ok = s_dve.n
            def f4(se):
                return se.activation(stdvA[:], bnagA[:, 1:2], AF.Ln,
                                     bias=eps_sb[:, 0:1], scale=1.0)
            b.op("act", f4, waits=[(s_dve, ag_ok)])
            def f4b(se, tt=tt):
                return se.activation(r1_all[:, tt:tt + 1], stdvA[:], AF.Exp,
                                     bias=0.0, scale=-0.5)
            b.op("act", f4b)
            def f4c(se, tt=tt):
                return se.activation(m1_all[:, tt:tt + 1], r1_all[:, tt:tt + 1],
                                     AF.Copy, bias=0.0, scale=bnagA[:, 0:1])
            b.op("act", f4c, inc=s_act)
            sc_done[tt] = s_act.n

        afm_vals = []
        for kc in range(NKC):
            def f(sp, kc=kc):
                return sp.dma_start_transpose(out=a_fm[:, kc * S:(kc + 1) * S],
                                              in_=at_d[:, kc * 128:(kc + 1) * 128])
            b.op("sp", f, inc=s_xfm, inc_amt=16)
            afm_vals.append(s_xfm.n)

        halves_loaded = {}
        buf_free = {}
        def emit_half3(j):
            c, hf = j // 2, j % 2
            buf = stb[j % 3]
            waits = []
            if j - 3 in buf_free:
                waits.append(buf_free[j - 3])
            def f(g, c=c, hf=hf, buf=buf):
                src = Wo[hf * 2048:(hf + 1) * 2048, c * 512:(c + 1) * 512] \
                    .rearrange("(kc p) c -> p kc c", p=128)
                dst = buf.ap().rearrange("p (kc c) -> p kc c", c=512)
                return g.dma_start(out=dst, in_=src)
            b.op("pool", f, waits=waits, inc=swb[j % 3], inc_amt=16)
            halves_loaded[j] = swb[j % 3].n
        emit_half3(0); emit_half3(1); emit_half3(2)

        last_store = {0: sst[0].n, 1: sst[1].n}
        it = 0
        for c in range(8):
            for t in range(NT):
                bank = ps[it % 4]
                waits = [(s_xfm, afm_vals[-1]),
                         (swb[(2 * c) % 3], halves_loaded[2 * c]),
                         (swb[(2 * c + 1) % 3], halves_loaded[2 * c + 1])]
                if it >= 4:
                    waits.append((s_ev, it - 3))
                for kc in range(NKC):
                    buf = stb[(2 * c + kc // 16) % 3]
                    def f(te, buf=buf, kc=kc, t=t, bank=bank):
                        return te.matmul(bank[:],
                                         a_fm[:, kc * S + t * 128: kc * S + t * 128 + 128],
                                         buf[:, (kc % 16) * 512:(kc % 16 + 1) * 512],
                                         start=(kc == 0), stop=(kc == NKC - 1))
                    if kc == 0:
                        b.op("pe", f, waits=waits)
                    elif kc == 15 and t == NT - 1:
                        b.op("pe", f, inc=s_pe)
                        buf_free[2 * c] = (s_pe, s_pe.n)
                    elif kc == NKC - 1:
                        b.op("pe", f, inc=s_pe)
                        if t == NT - 1:
                            buf_free[2 * c + 1] = (s_pe, s_pe.n)
                    else:
                        b.op("pe", f)
                mm_done = s_pe.n
                qb = ost[it % 2]
                w2_ = [(s_pe, mm_done), (sst[it % 2], last_store[it % 2]),
                       (sx3[it % 2], xld_ok[it])]
                def fy(v, bank=bank, qb=qb, bi=it % 2):
                    return v.tensor_tensor(qb[:], xr3[bi][:], bank[:], ALU.add)
                b.op("dve", fy, waits=w2_, inc=s_ev)
                ev_ok = s_ev.n
                xld_free[it] = ev_ok
                def fst_(v, qb=qb, t=t, c=c):
                    return v.bn_stats(bnstA[:, t * 48 + c * 6: t * 48 + c * 6 + 6], qb[:])
                b.op("dve", fst_)
                if it + 2 < 128:
                    emit_xload(it + 2)
                def f2(sp, qb=qb, t=t, c=c):
                    return sp.dma_start(out=wo_d[t * 128:(t + 1) * 128, c * 512:(c + 1) * 512],
                                        in_=qb[:])
                b.op("sp", f2, waits=[(s_ev, ev_ok)], inc=sst[it % 2], inc_amt=16)
                last_store[it % 2] = sst[it % 2].n
                if c == 7 and t >= 1:
                    emit_ln1_scalars(t - 1)
                it += 1
            nj = 2 * c + 3
            if nj < 16:
                emit_half3(nj)
            nj = 2 * c + 4
            if nj < 16:
                emit_half3(nj)
        def fsp3(v):
            return v.tensor_copy(xr3[0][:], xr3[1][:])
        b.op("dve", fsp3)
        emit_ln1_scalars(NT - 1)
        b.op("sp", None, waits=[(sst[0], sst[0].n), (sst[1], sst[1].n)])
        b.barrier(bar)

    if phases <= 3:
        return finish(nc, b, es, [])

    es2 = ExitStack()
    x1f_fm = es2.enter_context(nc.sbuf_tensor("x1f_fm", [128, 8 * S], BF16))
    xa1_bf = es2.enter_context(nc.sbuf_tensor("xa1_bf", [2, S], BF16))
    xa3_bf = es2.enter_context(nc.sbuf_tensor("xa3_bf", [2, S], BF16))

    with ExitStack() as ph:
        wo_sb = [ph.enter_context(nc.sbuf_tensor(f"wo_sb{i}", [128, D], F32)) for i in range(2)]
        x1_sb = [ph.enter_context(nc.sbuf_tensor(f"x1_sb{i}", [128, D], BF16)) for i in range(2)]
        x1f_t2 = [ph.enter_context(nc.sbuf_tensor(f"x1f_t{i}", [128, 1024], BF16)) for i in range(2)]
        x1fm_t2 = [ph.enter_context(nc.sbuf_tensor(f"x1fm_t{i}", [128, D], BF16)) for i in range(2)]
        a1_sb = ph.enter_context(nc.sbuf_tensor("a1_sb", [2, D], BF16))
        a3_sb = ph.enter_context(nc.sbuf_tensor("a3_sb", [2, D], BF16))
        a1T_sb = ph.enter_context(nc.sbuf_tensor("a1T_sb", [128, 64], BF16))
        a3T_sb = ph.enter_context(nc.sbuf_tensor("a3T_sb", [128, 64], BF16))
        xa1_sb = ph.enter_context(nc.sbuf_tensor("xa1_sb", [2, S], F32))
        xa3_sb = ph.enter_context(nc.sbuf_tensor("xa3_sb", [2, S], F32))
        tp_ps = [ph.enter_context(nc.psum_tensor(f"tp_ps{i}", [128, 1024], BF16)) for i in range(2)]
        aT_ps = ph.enter_context(nc.psum_tensor("aT_ps", [128, 128], BF16))
        xa_ps = ph.enter_context(nc.psum_tensor("xa_ps", [2, 512], F32))

        for asb, src in ((a1_sb, a1), (a3_sb, a3)):
            def f(g, asb=asb, src=src):
                return g.dma_start(out=asb[:], in_=src)
            b.op("pool", f, inc=sa13, inc_amt=16)
        a_ok = sa13.n
        for i, (asb, aTsb) in enumerate(((a1_sb, a1T_sb), (a3_sb, a3T_sb))):
            for kc in range(NKC):
                wl = []
                if kc == 0:
                    wl = [(sa13, a_ok)] if i == 0 else [(s_act, s_act.n)]
                def ft(te, asb=asb, kc=kc):
                    return te.transpose(aT_ps[:, kc * 2: kc * 2 + 2],
                                        asb[:, kc * 128:(kc + 1) * 128],
                                        id_sb[0:2, 0:2])
                b.op("pe", ft, waits=wl, inc=(s_pe if kc == NKC - 1 else None))
            trd = s_pe.n
            def fe(se, aTsb=aTsb):
                return se.activation(aTsb[:], aT_ps[:, 0:64], AF.Copy, bias=0.0, scale=1.0)
            b.op("act", fe, waits=[(s_pe, trd)], inc=s_act)

        y_done = {}
        x1_spill = {}
        tr_done = {}
        prev_fe2 = 0
        prev_xa_ev = 0
        for t in range(NT):
            wsb, x1b = wo_sb[t % 2], x1_sb[t % 2]
            wl = []
            if t >= 2:
                wl = [(s_dve, y_done[t - 2]), (sx1s[t % 2], x1_spill[t - 2])]
            def fl1(sp, wsb=wsb, t=t):
                return sp.dma_start(out=wsb[:], in_=wo_d[t * 128:(t + 1) * 128, :])
            b.op("sp", fl1, waits=wl, inc=sld3[t % 2], inc_amt=16)
            ld_ok = sld3[t % 2].n
            y_sb = wsb
            wl6 = [(sld3[t % 2], ld_ok)]
            if t >= 2:
                wl6.append((s_pe, tr_done[t - 2]))
            def f6(v, x1b=x1b, y_sb=y_sb, t=t):
                return v.tensor_scalar(x1b[:], y_sb[:], r1_all[:, t:t + 1],
                                       m1_all[:, t:t + 1],
                                       ALU.mult, ALU.subtract)
            b.op("dve", f6, waits=wl6, inc=s_dve)
            y_done[t] = s_dve.n
            x1_ok = s_dve.n
            def f7(sp, x1b=x1b, t=t):
                return sp.dma_start(out=x1_d[t * 128:(t + 1) * 128, :], in_=x1b[:])
            b.op("sp", f7, waits=[(s_dve, x1_ok)], inc=sx1s[t % 2], inc_amt=16)
            x1_spill[t] = sx1s[t % 2].n
            x1f_t = x1f_t2[t % 2]
            x1fm_t = x1fm_t2[t % 2]
            evs = []
            for kc in range(NKC):
                wv = kc // 8
                bank = tp_ps[wv % 2]
                wl = []
                if kc % 8 == 0:
                    wl = [(s_dve, x1_ok)]
                    if wv >= 2:
                        wl.append((s_act, evs[wv - 2]))
                    elif t > 0 and wv < 2:
                        wl.append((s_act, prev_fe2))
                def ft(te, kc=kc, x1b=x1b, bank=bank):
                    return te.transpose(bank[:, (kc % 8) * 128:(kc % 8 + 1) * 128],
                                        x1b[:, kc * 128:(kc + 1) * 128], id_sb[:])
                b.op("pe", ft, waits=wl, inc=(s_pe if kc % 8 == 7 else None))
                if kc % 8 == 7:
                    trd = s_pe.n
                    def fe(se, wv=wv, bank=bank, x1fm_t=x1fm_t):
                        return se.activation(x1fm_t[:, wv * 1024:(wv + 1) * 1024],
                                             bank[:], AF.Copy, bias=0.0, scale=1.0)
                    b.op("act", fe, waits=[(s_pe, trd)], inc=s_act)
                    evs.append(s_act.n)
            fm_ok = s_act.n
            tr_done[t] = s_pe.n
            for idx, (aTsb, xasb) in enumerate(((a1T_sb, xa1_sb), (a3T_sb, xa3_sb))):
                for kc in range(NKC):
                    wl = []
                    if kc == 0:
                        wl = [(s_act, fm_ok)]
                        if idx == 1 or t > 0:
                            wl.append((s_dve, prev_xa_ev))
                    def fx(te, aTsb=aTsb, kc=kc, x1fm_t=x1fm_t):
                        return te.matmul(xa_ps[:, 0:128],
                                         aTsb[:, kc * 2: kc * 2 + 2],
                                         x1fm_t[:, kc * 128:(kc + 1) * 128],
                                         start=(kc == 0), stop=(kc == NKC - 1))
                    b.op("pe", fx, waits=wl, inc=(s_pe if kc == NKC - 1 else None))
                xd = s_pe.n
                def fxe(v, xasb=xasb, t=t):
                    return v.tensor_copy(xasb[:, t * 128:(t + 1) * 128], xa_ps[:, 0:128])
                b.op("dve", fxe, waits=[(s_pe, xd)], inc=s_dve)
                prev_xa_ev = s_dve.n
            def ff(v, x1b=x1b, x1f_t=x1f_t):
                v.tensor_tensor(x1f_t[:], x1b[:, 0:1024], x1b[:, 1024:2048], ALU.add)
                v.tensor_tensor(x1f_t[:], x1f_t[:], x1b[:, 2048:3072], ALU.add)
                return v.tensor_tensor(x1f_t[:], x1f_t[:], x1b[:, 3072:4096], ALU.add)
            b.op("dve", ff, inc=s_dve)
            fold_ok = s_dve.n
            for jc in range(8):
                wl = []
                if jc == 0:
                    wl = [(s_dve, fold_ok), (s_act, s_act.n)]
                def ft(te, jc=jc, x1f_t=x1f_t):
                    return te.transpose(tp_ps[0][:, jc * 128:(jc + 1) * 128],
                                        x1f_t[:, jc * 128:(jc + 1) * 128], id_sb[:])
                b.op("pe", ft, waits=wl, inc=(s_pe if jc == 7 else None))
            trd = s_pe.n
            def fe2(se, t=t):
                dst = x1f_fm.ap().rearrange("p (jc s) -> p jc s", s=S)[:, :, t * 128:(t + 1) * 128]
                src = tp_ps[0].ap().rearrange("p (jc c) -> p jc c", c=128)
                return se.activation(dst, src, AF.Copy, bias=0.0, scale=1.0)
            b.op("act", fe2, waits=[(s_pe, trd)], inc=s_act)
            prev_fe2 = s_act.n
        xa_all = s_dve.n
        def fc1(se):
            return se.activation(xa1_bf[:], xa1_sb[:], AF.Copy, bias=0.0, scale=0.5)
        b.op("act", fc1, waits=[(s_dve, xa_all)], inc=s_act)
        def fc3(se):
            return se.activation(xa3_bf[:], xa3_sb[:], AF.Copy, bias=0.0, scale=0.5)
        b.op("act", fc3, inc=s_act)
        b.op("sp", None, waits=[(sx1s[0], sx1s[0].n), (sx1s[1], sx1s[1].n)])
        b.barrier(bar)

    if phases <= 4:
        return finish(nc, b, es, [es2])

    es3 = ExitStack()
    h_fold = es3.enter_context(nc.sbuf_tensor("h_fold", [128, NT * FQ], BF16, side="right"))
    pstg = es3.enter_context(nc.sbuf_tensor("pstg", [128, NT * 44], F32, side="right"))

    widths = [512] * 21 + [256]
    with ExitStack() as ph:
        st1 = [ph.enter_context(nc.sbuf_tensor(f"st1_{i}", [128, 8 * 512], BF16)) for i in range(2)]
        st3 = [ph.enter_context(nc.sbuf_tensor(f"st3_{i}", [128, 8 * 512], BF16)) for i in range(2)]
        b1s = [ph.enter_context(nc.sbuf_tensor(f"b1s{i}", [2, 512], BF16)) for i in range(2)]
        b3s = [ph.enter_context(nc.sbuf_tensor(f"b3s{i}", [2, 512], BF16)) for i in range(2)]
        a2r0 = [ph.enter_context(nc.sbuf_tensor(f"a2r0_{i}", [1, 512], BF16)) for i in range(2)]
        a2r1 = [ph.enter_context(nc.sbuf_tensor(f"a2r1_{i}", [1, 512], BF16)) for i in range(2)]
        a2rep = [ph.enter_context(nc.sbuf_tensor(f"a2rep{i}", [128, 1024], BF16)) for i in range(2)]
        ones_sb = ph.enter_context(nc.sbuf_tensor("ones_sb", [1, 128], BF16))
        sg_sb = [ph.enter_context(nc.sbuf_tensor(f"sg_sb{i}", [128, 512], BF16)) for i in range(2)]
        th_sb = [ph.enter_context(nc.sbuf_tensor(f"th_sb{i}", [128, 512], BF16)) for i in range(2)]
        pacc = ph.enter_context(nc.sbuf_tensor("pacc", [128, 1], F32))
        sdump = ph.enter_context(nc.sbuf_tensor("sdump", [128, 1024], BF16))
        g_ps = [ph.enter_context(nc.psum_tensor(f"g_ps{i}", [128, 512], F32)) for i in range(2)]
        u_ps = [ph.enter_context(nc.psum_tensor(f"u_ps{i}", [128, 512], F32)) for i in range(2)]
        ar_ps = ph.enter_context(nc.psum_tensor("ar_ps", [128, 512], F32))

        def fo_(v):
            return v.memset(ones_sb[:], 1.0)
        b.op("dve", fo_, inc=s_dve)
        ones_ok = s_dve.n

        ld_vals = {}
        stripe_used = {}
        def emit_stripe_load(fs):
            w = widths[fs]
            g0 = fs * 512
            bufi = fs % 2
            waits = []
            if fs >= 2:
                waits = [(s_pe, stripe_used[fs - 2])]
            def fA(g, fs=fs, w=w, g0=g0, bufi=bufi):
                src1 = w1qT[:, g0:g0 + w].rearrange("(jc p) c -> p jc c", p=128)
                d1 = st1[bufi].ap().rearrange("p (jc c) -> p jc c", c=512)[:, :, 0:w]
                return g.dma_start(out=d1, in_=src1)
            b.op("pool", fA, waits=waits, inc=s6w[c % 2], inc_amt=16)
            def fB(g, fs=fs, w=w, g0=g0, bufi=bufi):
                src3 = w3qT[:, g0:g0 + w].rearrange("(jc p) c -> p jc c", p=128)
                d3 = st3[bufi].ap().rearrange("p (jc c) -> p jc c", c=512)[:, :, 0:w]
                return g.dma_start(out=d3, in_=src3)
            b.op("pool", fB, inc=s4w[fs % 2], inc_amt=16)
            for dst, src in ((b1s[bufi], b1T), (b3s[bufi], b3T)):
                def fC(g, dst=dst, src=src, g0=g0, w=w):
                    return g.dma_start(out=dst[:, 0:w], in_=src[:, g0:g0 + w])
                b.op("pool", fC, inc=s4w[fs % 2], inc_amt=16)
            def fD(g, bufi=bufi, g0=g0, w=w):
                return g.dma_start(out=a2r0[bufi][:, 0:w], in_=a2[0:1, g0:g0 + w])
            b.op("pool", fD, inc=s4w[fs % 2], inc_amt=16)
            def fE(g, bufi=bufi, g0=g0, w=w):
                return g.dma_start(out=a2r1[bufi][:, 0:w], in_=a2[1:2, g0:g0 + w])
            b.op("pool", fE, inc=s4w[fs % 2], inc_amt=16)
            ld_vals[fs] = s4w[fs % 2].n

        emit_stripe_load(0)
        emit_stripe_load(1)

        silu_done = {}
        h_done = {}
        stt_done = {}
        a2rep_free = {}
        prev_arep_ev = 0
        for fs in range(22):
            w = widths[fs]
            g0 = fs * 512
            bufi = fs % 2
            wl = [(s4w[fs % 2], ld_vals[fs]), (s_dve, ones_ok)]
            if fs >= 2:
                wl.append((s_pool, a2rep_free[fs - 2]))
            if fs >= 1:
                wl.append((s_act, prev_arep_ev))
            def fb(te, bufi=bufi, w=w):
                return te.matmul(ar_ps[:, 0:w], ones_sb[:], a2r0[bufi][:, 0:w],
                                 start=True, stop=True)
            b.op("pe", fb, waits=wl, inc=s_pe)
            bd = s_pe.n
            def fbe(se, bufi=bufi, w=w):
                return se.activation(a2rep[bufi][:, 0:w], ar_ps[:, 0:w], AF.Copy,
                                     bias=0.0, scale=1.0)
            b.op("act", fbe, waits=[(s_pe, bd)], inc=s_act)
            def fb1(te, bufi=bufi, w=w):
                return te.matmul(ar_ps[:, 0:w], ones_sb[:], a2r1[bufi][:, 0:w],
                                 start=True, stop=True)
            b.op("pe", fb1, waits=[(s_act, s_act.n)], inc=s_pe)
            bd = s_pe.n
            def fbe1(se, bufi=bufi, w=w):
                return se.activation(a2rep[bufi][:, 512:512 + w], ar_ps[:, 0:w], AF.Copy,
                                     bias=0.0, scale=1.0)
            b.op("act", fbe1, waits=[(s_pe, bd)], inc=s_act)
            prev_arep_ev = s_act.n
            arep_ok = s_act.n

            for t in range(NT):
                itg = fs * NT + t
                gb, ub = g_ps[t % 2], u_ps[t % 2]
                sg, th = sg_sb[t % 2], th_sb[t % 2]
                wl = [(s4w[fs % 2], ld_vals[fs])]
                if itg >= 2:
                    wl.append((s_act, silu_done[itg - 2]))
                for jc in range(8):
                    def f(te, jc=jc, t=t, w=w, gb=gb, bufi=bufi):
                        return te.matmul(gb[:, 0:w],
                                         x1f_fm[:, jc * S + t * 128: jc * S + t * 128 + 128],
                                         st1[bufi][:, jc * 512: jc * 512 + w],
                                         start=(jc == 0), stop=False)
                    b.op("pe", f, waits=(wl if jc == 0 else []))
                def flo(te, t=t, w=w, gb=gb, bufi=bufi):
                    return te.matmul(gb[:, 0:w], xa1_bf[:, t * 128:(t + 1) * 128],
                                     b1s[bufi][:, 0:w], start=False, stop=True)
                b.op("pe", flo, inc=s_pe)
                g_done = s_pe.n
                wl = []
                if itg >= 2:
                    wl = [(s_dve, h_done[itg - 2])]
                for jc in range(8):
                    def f(te, jc=jc, t=t, w=w, ub=ub, bufi=bufi):
                        return te.matmul(ub[:, 0:w],
                                         x1f_fm[:, jc * S + t * 128: jc * S + t * 128 + 128],
                                         st3[bufi][:, jc * 512: jc * 512 + w],
                                         start=(jc == 0), stop=False)
                    b.op("pe", f, waits=(wl if jc == 0 else []))
                def flo3(te, t=t, w=w, ub=ub, bufi=bufi):
                    return te.matmul(ub[:, 0:w], xa3_bf[:, t * 128:(t + 1) * 128],
                                     b3s[bufi][:, 0:w], start=False, stop=True)
                b.op("pe", flo3, inc=s_pe)
                u_done = s_pe.n
                wl = [(s_pe, g_done)]
                if itg >= 2:
                    wl.append((s_dve, h_done[itg - 2]))
                def fsi(se, sg=sg, gb=gb, w=w):
                    return se.activation(sg[:, 0:w], gb[:, 0:w], AF.Silu, bias=0.0, scale=1.0)
                b.op("act", fsi, waits=wl, inc=s_act)
                silu_done[itg] = s_act.n
                wl = [(s_pe, u_done), (s_act, silu_done[itg])]
                if itg >= 2:
                    wl.append((s_pool, stt_done[itg - 2]))
                def fh(v, th=th, sg=sg, ub=ub, w=w):
                    return v.tensor_tensor(th[:, 0:w], ub[:, 0:w], sg[:, 0:w], ALU.mult)
                b.op("dve", fh, waits=wl, inc=s_dve)
                hm_done = s_dve.n
                col = g0; rem = w; off = 0
                while rem > 0:
                    cc = col % FQ
                    run = min(rem, FQ - cc)
                    cpy = (col // FQ == 0)
                    def fsc(v, t=t, cc=cc, off=off, run=run, th=th, cpy=cpy):
                        dst = h_fold[:, t * FQ + cc: t * FQ + cc + run]
                        if cpy:
                            return v.tensor_copy(dst, th[:, off:off + run])
                        return v.tensor_tensor(dst, dst, th[:, off:off + run], ALU.add)
                    b.op("dve", fsc)
                    col += run; off += run; rem -= run
                b.op("dve", None, inc=s_dve)
                h_done[itg] = s_dve.n
                def fstt(p, th=th, w=w, bufi=bufi, t=t, fs=fs):
                    p.tensor_tensor(sdump[:, 0:w], th[:, 0:w], a2rep[bufi][:, 0:w], ALU.mult)
                    p.tensor_tensor(sdump[:, 512:512 + w], th[:, 0:w],
                                    a2rep[bufi][:, 512:512 + w], ALU.mult)
                    p.tensor_reduce(pstg[:, t * 44 + fs: t * 44 + fs + 1],
                                    sdump[:, 0:w], AX.X, ALU.add)
                    return p.tensor_reduce(pstg[:, t * 44 + 22 + fs: t * 44 + 22 + fs + 1],
                                           sdump[:, 512:512 + w], AX.X, ALU.add)
                b.op("dve", fstt, waits=[(s_dve, hm_done), (s_act, arep_ok)], inc=s_pool)
                stt_done[itg] = s_pool.n
            stripe_used[fs] = s_pe.n
            a2rep_free[fs] = s_pool.n
            if fs + 2 < 22:
                emit_stripe_load(fs + 2)
        b.barrier(bar)

    if dbg:
        for t in range(NT):
            def fdbg(sp, t=t):
                return sp.dma_start(out=hf_dbg[t * 128:(t + 1) * 128, :],
                                    in_=h_fold[:, t * FQ:(t + 1) * FQ])
            b.op("sp", fdbg, inc=sem_s, inc_amt=16)
        def fdbg2(sp):
            return sp.dma_start(out=x1f_dbg.rearrange("j p s -> p j s"),
                                in_=x1f_fm.ap().rearrange("p (j s) -> p j s", s=S))
        b.op("sp", fdbg2, inc=sem_s, inc_amt=16)
        def fdbg3(sp):
            return sp.dma_start(out=xab_dbg[0:2, :], in_=xa1_bf[:])
        b.op("sp", fdbg3, inc=sem_s, inc_amt=16)
        def fdbg3b(sp):
            return sp.dma_start(out=xab_dbg[2:4, :], in_=xa3_bf[:])
        b.op("sp", fdbg3b, inc=sem_s, inc_amt=16)
        def fdbg4(sp):
            return sp.dma_start(out=xaa_dbg2, in_=pstg[:])
        b.op("sp", fdbg4, inc=sem_s, inc_amt=16)
        for i_ in range(2):
            def fdbg5(sp, i_=i_):
                return sp.dma_start(out=arep_dbg[i_], in_=a2rep[i_][:])
            b.op("sp", fdbg5, inc=sem_s, inc_amt=16)
        b.op("sp", None, waits=[(sem_s, sem_s.n)])
        b.barrier(bar)

    if phases <= 5:
        return finish(nc, b, es, [es2, es3])

    es2.close()
    es4 = ExitStack()
    h_f_fm = es4.enter_context(nc.sbuf_tensor("h_f_fm", [128, 22 * S], BF16))
    xa2_bf = es4.enter_context(nc.sbuf_tensor("xa2_bf", [2, S], BF16))

    with ExitStack() as ph:
        tp_ps = [ph.enter_context(nc.psum_tensor(f"tp5_{i}", [128, 1024], BF16)) for i in range(2)]
        xaT_ps = ph.enter_context(nc.psum_tensor("xaT_ps", [2, 1024], BF16))
        xa2c = ph.enter_context(nc.sbuf_tensor("xa2c", [128, NT * 2], BF16))

        prev_ev = 0
        for t in range(NT):
            evs = []
            for jc in range(22):
                wdt = min(128, FQ - jc * 128)
                wv = jc // 8
                bank = tp_ps[wv % 2]
                wl = []
                if jc % 8 == 0:
                    if wv >= 2:
                        wl = [(s_act, evs[wv - 2])]
                    elif t > 0:
                        wl = [(s_act, prev_ev)]
                def ft(te, t=t, jc=jc, wdt=wdt, bank=bank):
                    return te.transpose(bank[0:wdt, (jc % 8) * 128:(jc % 8) * 128 + 128],
                                        h_fold[:, t * FQ + jc * 128: t * FQ + jc * 128 + wdt],
                                        id_sb[:])
                b.op("pe", ft, waits=wl, inc=s_pe)
                if jc % 8 == 7 or jc == 21:
                    trd = s_pe.n
                    jc0 = 8 * wv
                    nfull = min(jc - jc0 + 1, 21 - jc0 + (1 if jc < 21 else 0))
                    if jc == 21:
                        def feA(se, t=t, jc0=jc0, bank=bank):
                            dst = h_f_fm.ap().rearrange("p (jc s) -> p jc s", s=S)[
                                :, jc0:jc0 + 5, t * 128:(t + 1) * 128]
                            src = bank.ap().rearrange("p (jc c) -> p jc c", c=128)[:, 0:5, :]
                            return se.activation(dst, src, AF.Copy, bias=0.0, scale=1.0)
                        b.op("act", feA, waits=[(s_pe, trd)], inc=s_act)
                        def feB(se, t=t, bank=bank):
                            return se.activation(
                                h_f_fm[0:64, 21 * S + t * 128: 21 * S + t * 128 + 128],
                                bank[0:64, 5 * 128:6 * 128], AF.Copy, bias=0.0, scale=1.0)
                        b.op("act", feB, inc=s_act)
                    else:
                        n_in = (jc % 8) + 1
                        def fe(se, t=t, jc0=jc0, n=n_in, bank=bank):
                            dst = h_f_fm.ap().rearrange("p (jc s) -> p jc s", s=S)[
                                :, jc0:jc0 + n, t * 128:(t + 1) * 128]
                            src = bank.ap().rearrange("p (jc c) -> p jc c", c=128)[:, 0:n, :]
                            return se.activation(dst, src, AF.Copy, bias=0.0, scale=1.0)
                        b.op("act", fe, waits=[(s_pe, trd)], inc=s_act)
                    evs.append(s_act.n)
            prev_ev = s_act.n
        xa2f = ph.enter_context(nc.sbuf_tensor("xa2f", [128, NT * 2], F32))
        for t_ in range(NT):
            for r_ in range(2):
                def fcmb(v, t_=t_, r_=r_):
                    return v.tensor_reduce(xa2f[:, t_ * 2 + r_: t_ * 2 + r_ + 1],
                                           pstg[:, t_ * 44 + r_ * 22: t_ * 44 + r_ * 22 + 22],
                                           AX.X, ALU.add)
                b.op("dve", fcmb)
        b.op("dve", None, inc=s_dve)
        red_ok = s_dve.n
        def fccast(se):
            return se.activation(xa2c[:], xa2f[:], AF.Copy, bias=0.0, scale=1.0)
        b.op("act", fccast, waits=[(s_dve, red_ok)], inc=s_act)
        c_ok_act = s_act.n
        b.op("dve", None, waits=[(s_act, c_ok_act)], inc=s_dve)
        c_ok = s_dve.n
        for t in range(NT):
            wl = []
            if t == 0:
                wl = [(s_dve, c_ok)]
            elif t == 8:
                wl = [(s_act, s_act.n)]
            def ft(te, t=t):
                return te.transpose(xaT_ps[:, (t % 8) * 128:(t % 8) * 128 + 128],
                                    xa2c[:, t * 2:(t + 1) * 2], id_sb[:])
            b.op("pe", ft, waits=wl, inc=(s_pe if (t == 7 or t == 15) else None))
            if t == 7 or t == 15:
                trd = s_pe.n
                t0 = 8 * (t // 8)
                def fe(se, t0=t0):
                    return se.activation(xa2_bf[:, t0 * 128:(t0 + 8) * 128],
                                         xaT_ps[:, 0:1024], AF.Copy, bias=0.0, scale=0.5)
                b.op("act", fe, waits=[(s_pe, trd)], inc=s_act)
        b.barrier(bar)
    es3.close()

    if phases <= 6:
        return finish(nc, b, es, [es4])

    with ExitStack() as ph:
        st2 = [ph.enter_context(nc.sbuf_tensor(f"st2_{i}", [128, 22 * 512], BF16)) for i in range(2)]
        b2s = ph.enter_context(nc.sbuf_tensor("b2s", [2, D], BF16))
        fst = [ph.enter_context(nc.sbuf_tensor(f"fst{i}", [128, 512], F32)) for i in range(2)]
        x1c = [ph.enter_context(nc.sbuf_tensor(f"x1c{i}", [128, 512], BF16)) for i in range(2)]
        bnstB = ph.enter_context(nc.sbuf_tensor("bnstB", [128, NT * 48], F32))
        bnagB = ph.enter_context(nc.sbuf_tensor("bnagB", [128, 2], F32))
        stdvB = ph.enter_context(nc.sbuf_tensor("stdvB", [128, 1], F32))
        ps = [ph.enter_context(nc.psum_tensor(f"p6_{i}", [128, 512], F32)) for i in range(4)]
        s_ev = b.sem("ev6")

        def fb2(g):
            return g.dma_start(out=b2s[:], in_=b2T)
        b.op("pool", fb2, inc=sb2, inc_amt=16)
        b2_ok = sb2.n

        x1ld_ok = {}
        x1ld_free = {}
        sc6_done = {}

        def emit_x1load(it2):
            c2, t2 = it2 // NT, it2 % NT
            wl = []
            if it2 >= 2:
                wl = [(s_ev, x1ld_free[it2 - 2])]
            def fx(g, c2=c2, t2=t2, bi=it2 % 2):
                return g.dma_start(out=x1c[bi][:],
                                   in_=x1_d[t2 * 128:(t2 + 1) * 128,
                                            c2 * 512:(c2 + 1) * 512])
            b.op("pool", fx, waits=wl, inc=sx6[it2 % 2], inc_amt=16)
            x1ld_ok[it2] = sx6[it2 % 2].n
        emit_x1load(0)
        emit_x1load(1)

        def emit_ln2_scalars(tt):
            wl = []
            if tt >= 1:
                wl = [(s_act, sc6_done[tt - 1])]
            def fa(v, tt=tt):
                return v.bn_aggr(bnagB[:], bnstB[:, tt * 48:(tt + 1) * 48])
            b.op("dve", fa, waits=wl, inc=s_dve)
            ag_ok = s_dve.n
            def f4(se):
                return se.activation(stdvB[:], bnagB[:, 1:2], AF.Ln,
                                     bias=eps_sb[:, 0:1], scale=1.0)
            b.op("act", f4, waits=[(s_dve, ag_ok)])
            def f4b(se, tt=tt):
                return se.activation(r2_all[:, tt:tt + 1], stdvB[:], AF.Exp,
                                     bias=0.0, scale=-0.5)
            b.op("act", f4b)
            def f4c(se, tt=tt):
                return se.activation(m2_all[:, tt:tt + 1], r2_all[:, tt:tt + 1],
                                     AF.Copy, bias=0.0, scale=bnagB[:, 0:1])
            b.op("act", f4c, inc=s_act)
            sc6_done[tt] = s_act.n

        ld_vals = {}
        s2_used = {}
        def emit_s2(c):
            bufi = c % 2
            waits = []
            if c >= 2:
                waits = [(s_pe, s2_used[c - 2])]
            def fA(g, c=c, bufi=bufi):
                src = w2qT[0:2688, c * 512:(c + 1) * 512].rearrange("(jc p) c -> p jc c", p=128)
                dst = st2[bufi].ap().rearrange("p (jc c) -> p jc c", c=512)[:, 0:21, :]
                return g.dma_start(out=dst, in_=src)
            b.op("pool", fA, waits=waits, inc=s6w[c % 2], inc_amt=16)
            def fB(g, c=c, bufi=bufi):
                return g.dma_start(out=st2[bufi][0:64, 21 * 512:22 * 512],
                                   in_=w2qT[2688:2752, c * 512:(c + 1) * 512])
            b.op("pool", fB, inc=s6w[c % 2], inc_amt=16)
            ld_vals[c] = s6w[c % 2].n
        emit_s2(0); emit_s2(1)

        last_store6 = {0: sst[0].n, 1: sst[1].n}
        it = 0
        for c in range(8):
            bufi = c % 2
            for t in range(NT):
                bank = ps[it % 4]
                wl = [(s6w[c % 2], ld_vals[c]), (sb2, b2_ok)]
                if it >= 4:
                    wl.append((s_ev, it - 3))
                for jc in range(22):
                    kw = min(128, FQ - jc * 128)
                    def f(te, jc=jc, t=t, kw=kw, bank=bank, bufi=bufi):
                        return te.matmul(bank[:],
                                         h_f_fm[0:kw, jc * S + t * 128: jc * S + t * 128 + 128],
                                         st2[bufi][0:kw, jc * 512:(jc + 1) * 512],
                                         start=(jc == 0), stop=False)
                    b.op("pe", f, waits=(wl if jc == 0 else []))
                def flo(te, t=t, c=c, bank=bank):
                    return te.matmul(bank[:], xa2_bf[:, t * 128:(t + 1) * 128],
                                     b2s[:, c * 512:(c + 1) * 512], start=False, stop=True)
                b.op("pe", flo, inc=s_pe)
                mm_done = s_pe.n
                fb = fst[it % 2]
                w2_ = [(s_pe, mm_done), (sst[it % 2], last_store6[it % 2]),
                       (sx6[it % 2], x1ld_ok[it])]
                def fy(v, bank=bank, fb=fb, bi=it % 2):
                    return v.tensor_tensor(fb[:], bank[:], x1c[bi][:], ALU.add)
                b.op("dve", fy, waits=w2_, inc=s_ev)
                ev_ok = s_ev.n
                x1ld_free[it] = ev_ok
                def fst_(v, fb=fb, t=t, c=c):
                    return v.bn_stats(bnstB[:, t * 48 + c * 6: t * 48 + c * 6 + 6], fb[:])
                b.op("dve", fst_)
                if it + 2 < 128:
                    emit_x1load(it + 2)
                def f2(sp, fb=fb, t=t, c=c):
                    return sp.dma_start(out=f_d[t * 128:(t + 1) * 128, c * 512:(c + 1) * 512],
                                        in_=fb[:])
                b.op("sp", f2, waits=[(s_ev, ev_ok)], inc=sst[it % 2], inc_amt=16)
                last_store6[it % 2] = sst[it % 2].n
                if c == 7 and t >= 1:
                    emit_ln2_scalars(t - 1)
                it += 1
            s2_used[c] = s_pe.n
            if c + 2 < 8:
                emit_s2(c + 2)
        def fsp6(v):
            return v.tensor_copy(x1c[0][:], x1c[1][:])
        b.op("dve", fsp6)
        emit_ln2_scalars(NT - 1)
        b.op("sp", None, waits=[(sst[0], sst[0].n), (sst[1], sst[1].n)])
        b.barrier(bar)
    es4.close()

    with ExitStack() as ph:
        f_sb = [ph.enter_context(nc.sbuf_tensor(f"f_sb{i}", [128, D], F32)) for i in range(2)]
        o_sb = [ph.enter_context(nc.sbuf_tensor(f"o_sb{i}", [128, D], F32)) for i in range(2)]

        o_done = {}
        o_store = {}
        for t in range(NT):
            fb, ob = f_sb[t % 2], o_sb[t % 2]
            wl = []
            if t >= 2:
                wl = [(s_dve, o_done[t - 2]), (so6[t % 2], o_store[t - 2])]
            def fl1(sp, fb=fb, t=t):
                return sp.dma_start(out=fb[:], in_=f_d[t * 128:(t + 1) * 128, :])
            b.op("sp", fl1, waits=wl, inc=sld6[t % 2], inc_amt=16)
            ld_ok = sld6[t % 2].n
            def f6(v, ob=ob, fb=fb, t=t):
                return v.tensor_scalar(ob[:], fb[:], r2_all[:, t:t + 1],
                                       m2_all[:, t:t + 1],
                                       ALU.mult, ALU.subtract)
            b.op("dve", f6, waits=[(sld6[t % 2], ld_ok)], inc=s_dve)
            o_done[t] = s_dve.n
            def f7(sp, ob=ob, t=t):
                return sp.dma_start(out=y_out[t * 128:(t + 1) * 128, :], in_=ob[:])
            b.op("sp", f7, waits=[(s_dve, o_done[t])], inc=so6[t % 2], inc_amt=16)
            o_store[t] = so6[t % 2].n
        b.op("sp", None, waits=[(so6[0], so6[0].n), (so6[1], so6[1].n)])

    return finish(nc, b, es, [])


def finish(nc, b, es, extra_stacks):
    for st in extra_stacks:
        try:
            st.close()
        except Exception:
            pass
    with ExitStack() as run:
        for s in b.sems:
            s.h = run.enter_context(nc.semaphore(s.name))
        block = run.enter_context(nc.Block())
        block.tensor(lambda e: b.replay("pe", e))
        block.scalar(lambda e: b.replay("act", e))
        block.vector(lambda e: b.replay("dve", e))
        block.gpsimd(lambda e: b.replay("pool", e))
        block.sync(lambda e: b.replay("sp", e))
    es.close()
    return nc


def _consts():
    i = np.arange(64, dtype=np.float64)
    invf = THETA ** (-2.0 * i / HD)
    t = np.arange(S, dtype=np.float64)
    ang = t[:, None] * invf[None, :]
    cos = np.cos(ang); sin = np.sin(ang)
    sc = 1.0 / np.sqrt(HD)
    ck = np.concatenate([cos, cos], axis=1)
    sk = np.concatenate([-sin, sin], axis=1)
    rkc_ = np.tile(ck, (1, 4)).astype(ml_dtypes.bfloat16)
    rks_ = np.tile(sk, (1, 4)).astype(ml_dtypes.bfloat16)
    rqct_ = (np.tile(ck, (1, 4)) * sc).astype(ml_dtypes.bfloat16)
    rqst_ = (np.tile(sk, (1, 4)) * sc).astype(ml_dtypes.bfloat16)
    ident = np.eye(128, dtype=ml_dtypes.bfloat16)
    mask = np.triu(np.full((128, 128), -1e30, dtype=np.float32), k=1)
    return rqct_, rqst_, rkc_, rks_, ident, mask


def make_in_maps(L0_attn_x, L0_attn_Wq, L0_attn_Wk, L0_attn_Wv, L0_attn_Wo,
                 L0_ffn_w1q, L0_ffn_a1, L0_ffn_b1, L0_ffn_w3q, L0_ffn_a3,
                 L0_ffn_b3, L0_ffn_w2q, L0_ffn_a2, L0_ffn_b2):
    x = np.ascontiguousarray(np.asarray(L0_attn_x, dtype=np.float32))
    rqct_, rqst_, rkc_, rks_, ident, mask = _consts()
    shared = {
        "Wq": np.ascontiguousarray(np.asarray(L0_attn_Wq, dtype=np.float32)),
        "Wk": np.ascontiguousarray(np.asarray(L0_attn_Wk, dtype=np.float32)),
        "Wv": np.ascontiguousarray(np.asarray(L0_attn_Wv, dtype=np.float32)),
        "Wo": np.ascontiguousarray(np.asarray(L0_attn_Wo, dtype=np.float32)),
        "w1qT": np.ascontiguousarray(np.asarray(L0_ffn_w1q, dtype=np.float32).T),
        "w3qT": np.ascontiguousarray(np.asarray(L0_ffn_w3q, dtype=np.float32).T),
        "w2qT": np.ascontiguousarray(np.asarray(L0_ffn_w2q, dtype=np.float32).T),
        "a1": np.ascontiguousarray(np.asarray(L0_ffn_a1, dtype=np.float32)),
        "a3": np.ascontiguousarray(np.asarray(L0_ffn_a3, dtype=np.float32)),
        "a2": np.ascontiguousarray(np.asarray(L0_ffn_a2, dtype=np.float32)),
        "b1T": np.ascontiguousarray(np.asarray(L0_ffn_b1, dtype=np.float32).T),
        "b3T": np.ascontiguousarray(np.asarray(L0_ffn_b3, dtype=np.float32).T),
        "b2T": np.ascontiguousarray(np.asarray(L0_ffn_b2, dtype=np.float32).T),
        "identc": ident,
        "maskb": np.ascontiguousarray(mask.T).astype(ml_dtypes.bfloat16),
        "rqct": rqct_, "rqst": rqst_, "rkc": rkc_, "rks": rks_,
    }
    in_maps = []
    for core in range(8):
        m = dict(shared)
        xb = np.ascontiguousarray(x[core])
        m["x"] = xb
        m["xT"] = np.ascontiguousarray(xb.T)
        in_maps.append(m)
    return in_maps


def kernel(**inputs):
    in_maps = make_in_maps(**inputs)
    nc = build_program()
    res = run_bass_kernel_spmd(nc, in_maps, core_ids=list(range(8)))
    return np.stack([np.asarray(r["y"], dtype=np.float32) for r in res.results], axis=0)
